# version 09
# baseline (speedup 1.0000x reference)
import numpy as np
import ml_dtypes
import concourse.bass as bass
import concourse.mybir as mybir
from concourse.bass_utils import run_bass_kernel_spmd

F32 = mybir.dt.float32
BF16 = mybir.dt.bfloat16
AF = mybir.ActivationFunctionType
ALU = mybir.AluOpType
AX = mybir.AxisListType

B_, SEQ, C = 4, 8192, 2048
H, D, T = 16, 128, 128
M = 4096
MT = 512
G = M // MT
NTT = MT // 128
NCH = MT // T
KC = C // 128
I = 8192
EPS32 = float(np.finfo(np.float32).eps)
SCALE = float(D) ** -0.5
N_CORES = 8
BF = ml_dtypes.bfloat16


def _host_prep(inputs):
    x = np.ascontiguousarray(np.asarray(inputs["L0_attn_x"], np.float32))
    qkv_w = np.asarray(inputs["L0_attn_qkv_w"], np.float32)
    qkv_norm_w = np.asarray(inputs["L0_attn_qkv_norm_w"], np.float32)
    proj_w = np.asarray(inputs["L0_attn_proj_w"], np.float32)
    proj_norm_w = np.asarray(inputs["L0_attn_proj_norm_w"], np.float32)
    norm_w = np.asarray(inputs["L0_attn_norm_w"], np.float32)
    s = np.asarray(inputs["L0_attn_s"], np.float32)
    w_gate = np.asarray(inputs["L0_ffn_w_gate"], np.float32)
    w_up = np.asarray(inputs["L0_ffn_w_up"], np.float32)
    w_down = np.asarray(inputs["L0_ffn_w_down"], np.float32)

    qkvT = np.ascontiguousarray(
        qkv_w[:4096].reshape(32, 128, KC, 128).transpose(0, 3, 2, 1).reshape(32, 128, C))
    vT = np.ascontiguousarray(qkv_w[4096:].T.reshape(KC, 128, C))
    pT = np.ascontiguousarray(
        proj_w.reshape(16, 128, KC, 128).transpose(0, 3, 2, 1).reshape(16, 128, C))
    gq = np.arange(16)[:, None, None]
    r4 = np.arange(4)[None, :, None]
    n128 = np.arange(128)[None, None, :]
    perm_cols = (gq * 512 + n128 * 4 + r4).reshape(-1)
    gT = np.ascontiguousarray(
        w_gate[:, perm_cols].reshape(KC, 128, 64, 128).transpose(2, 1, 0, 3).reshape(64, 128, C))
    uT = np.ascontiguousarray(
        w_up[:, perm_cols].reshape(KC, 128, 64, 128).transpose(2, 1, 0, 3).reshape(64, 128, C))
    c2 = np.arange(64)[:, None]
    p128 = np.arange(128)[None, :]
    rows_perm = ((c2 // 4) * 512 + p128 * 4 + (c2 % 4)).reshape(-1)
    wdp = w_down[rows_perm, :]
    d6 = wdp.reshape(4, 4, 4, 128, 2, 2, 512)
    dT = np.ascontiguousarray(
        d6.transpose(0, 4, 1, 3, 2, 5, 6).reshape(32, 128, 4096))

    qnw = np.ascontiguousarray(qkv_norm_w.reshape(KC, 128).T)
    nw16 = np.ascontiguousarray(norm_w.reshape(KC, 128).T)
    pnw16 = np.ascontiguousarray(proj_norm_w.reshape(KC, 128).T)
    nwrow = np.ascontiguousarray(norm_w[None, :])
    srow = np.ascontiguousarray(s[None, :])

    ar = np.arange(128, dtype=np.float32)
    consts = {
        "identb": np.eye(128, dtype=np.float32).astype(BF),
        "identf": np.eye(128, dtype=np.float32),
        "diffm": np.maximum(ar[None, :] - ar[:, None], 0.0).astype(np.float32),
        "tril": (ar[None, :] >= ar[:, None]).astype(np.float32),
        "rampc": ar.reshape(128, 1).copy(),
        "ramprc": (T - ar).reshape(128, 1).astype(np.float32),
        "rampc16": np.tile(ar.reshape(128, 1), (1, 16)).astype(np.float32),
        "ramprc16": np.tile((T - ar).reshape(128, 1), (1, 16)).astype(np.float32),
        "ones1rf": np.ones((1, 128), np.float32),
        "onescf": np.ones((128, 1), np.float32),
        "fconsts": np.tile(np.array([[1e-5, 1e-6, EPS32, -0.5, -1.0, -128.0, SCALE, 0.0]],
                                    np.float32), (128, 1)),
    }
    shared = dict(qkvT=qkvT, vT=vT, pT=pT, gT=gT, uT=uT, dT=dT, qnw=qnw, nw16=nw16,
                  pnw16=pnw16, nwrow=nwrow, srow=srow, **consts)
    in_maps = []
    for core in range(N_CORES):
        b, hf = core // 2, core % 2
        xc = np.ascontiguousarray(x[b, hf * M:(hf + 1) * M])
        xp = (np.ascontiguousarray(x[b, 0:M]) if hf == 1 else np.zeros((M, C), np.float32))
        in_maps.append(dict(x=xc, xprev=xp, **shared))
    return in_maps


class _Sem:
    def __init__(self, h):
        self.h = h
        self.n = 0

    def tag(self):
        return (self, self.n)


class KB:
    def __init__(self, n_prol=G, n_main=G, scope="full", dbg=(), scratch_kind=None):
        self.dbg = set(dbg)
        self.n_prol = n_prol
        self.n_main = n_main
        self.scope = scope
        nc = self.nc = bass.Bass("TRN2", target_bir_lowering=False, debug=False)
        self.st = {k: [] for k in ("sync", "tensor", "vector", "scalar", "gpsimd")}

        dram = {}
        def din(name, shape, dt=F32):
            dram[name] = nc.dram_tensor(name, list(shape), dt, kind="ExternalInput").ap()
        din("x", (M, C)); din("xprev", (M, C))
        din("qkvT", (32, 128, C)); din("vT", (KC, 128, C)); din("pT", (KC, 128, C))
        din("gT", (64, 128, C)); din("uT", (64, 128, C)); din("dT", (32, 128, 4096))
        din("qnw", (128, 16)); din("nw16", (128, 16)); din("pnw16", (128, 16))
        din("nwrow", (1, C)); din("srow", (1, 16))
        din("identb", (128, 128), BF16); din("identf", (128, 128))
        din("diffm", (128, 128)); din("tril", (128, 128))
        din("rampc", (128, 1)); din("ramprc", (128, 1))
        din("rampc16", (128, 16)); din("ramprc16", (128, 16))
        din("ones1rf", (1, 128)); din("onescf", (128, 1))
        din("fconsts", (128, 8))
        self.dr = dram
        self.out = nc.dram_tensor("out", [M, C], F32, kind="ExternalOutput").ap()
        ckind = "Internal" if scratch_kind is None else scratch_kind
        self.cqkv = nc.dram_tensor("cqkv", [32, 128, C], BF16, kind=ckind).ap()
        self.cv = nc.dram_tensor("cv", [KC, 128, C], BF16, kind=ckind).ap()
        self.cpw = nc.dram_tensor("cpw", [KC, 128, C], BF16, kind=ckind).ap()
        self.cgw = nc.dram_tensor("cgw", [64, 128, C], BF16, kind=ckind).ap()
        self.cuw = nc.dram_tensor("cuw", [64, 128, C], BF16, kind=ckind).ap()
        self.cdw = nc.dram_tensor("cdw", [32, 128, 4096], BF16, kind=ckind).ap()
        self.x2s = nc.dram_tensor("x2s", [M, C], F32, kind=ckind).ap()
        self.bnc1 = nc.dram_tensor("bnc1", [16, 128], F32, kind=ckind).ap()
        self.bnc2 = nc.dram_tensor("bnc2", [16, 128], F32, kind=ckind).ap()

        self.cur = (int(nc.sbuf_base) + 31) // 32 * 32
        def sb(name, shape, dt=F32):
            nbytes = int(np.prod(shape[1:])) * (2 if dt == BF16 else 4)
            t = nc.alloc_sbuf_tensor_at(name, list(shape), dt, offset=self.cur)
            self.cur += (nbytes + 31) // 32 * 32
            return t
        def sb_at(name, shape, dt, off):
            return nc.alloc_sbuf_tensor_at(name, list(shape), dt, offset=off)

        self.identb = sb("s_identb", (128, 128), BF16)
        self.identf = sb("s_identf", (128, 128))
        self.diffm = sb("s_diffm", (128, 128))
        self.tril = sb("s_tril", (128, 128))
        self.masksT = sb("s_masksT", (128, H * 128), BF16)
        self.lam_t_bc = sb("s_lam_t_bc", (128, H * 128), BF16)
        self.nw2_bc = sb("s_nw2_bc", (128, C))
        self.S = sb("s_S", (128, H * 128))
        self.Sbf = [sb("s_Sbf0", (128, H * 128), BF16),
                    sb("s_Sbf1", (128, H * 128), BF16)]
        for nm in ("s_bc", "nsbc", "lam_t16", "lam_rev16", "lamT16",
                   "qnw_s", "nw_s", "pnw_s", "rfq", "rfq2", "rfp", "rampc16_s", "ramprc16_s"):
            setattr(self, nm, sb("s_" + nm, (128, 16)))
        for nm in ("rampc_s", "ramprc_s", "onescf_s", "accq", "accp", "accqB", "accpB",
                   "redt", "redtB", "mq", "mp", "thrq", "nthrq", "thrp", "nthrp"):
            setattr(self, nm, sb("s_" + nm, (128, 1)))
        self.fconsts = sb("s_fconsts", (128, 8))
        for _j, _v in enumerate((1e-5, 1e-6, EPS32, -0.5, -1.0, -128.0, SCALE)):
            nc.const_aps.register(F32, float(_v), self.fconsts[:, _j:_j + 1])
        self.ones1rf_s = sb("s_ones1rf", (1, 128))
        self.srow_s = sb("s_srow", (1, 16))
        self.stot = sb("s_stot", (1, 1))
        self.stn = [sb(f"s_st{i}", (128, 1)) for i in range(12)]

        self.off_U1 = self.cur; self.cur += 73728
        self.off_FM2 = self.cur; self.cur += 16384
        self.off_XT = self.cur; self.cur += 8192
        self.off_TRASH = self.cur; self.cur += 8192
        self.off_WA = self.cur; self.cur += 16384
        self.off_WB = self.cur; self.cur += 32768
        self.off_SMB = self.cur; self.cur += 8192
        self.off_ATTS = self.cur; self.cur += 8192
        assert self.cur <= int(nc.sbuf_top), f"SBUF over budget: {self.cur} > {nc.sbuf_top}"
        U1, FM2, XT, TR = self.off_U1, self.off_FM2, self.off_XT, self.off_TRASH
        WA, WB, SMB, AT = self.off_WA, self.off_WB, self.off_SMB, self.off_ATTS

        self.h_fm = sb_at("v_h_fm", (128, KC * MT), BF16, U1)
        self.q_fm = sb_at("v_q_fm", (128, H * MT), BF16, U1 + 16384)
        self.k_fm = sb_at("v_k_fm", (128, H * MT), BF16, U1 + 32768)
        self.v_tm = sb_at("v_v_tm", (128, H * MT), BF16, U1 + 49152)
        self.o_sb = [sb_at("v_o0", (128, C), F32, U1),
                     sb_at("v_o1", (128, C), F32, U1 + 8192)]
        self.attn_fm = sb_at("v_atfm", (128, 16 * MT), F32, U1)
        self.attn_sb = [sb_at("v_at0", (128, C), F32, U1 + 32768),
                        sb_at("v_at1", (128, C), F32, U1 + 40960)]
        self.x2t = sb_at("v_x2t", (128, C), F32, U1 + 49152)
        self.h_sbuf = sb_at("v_hsb", (128, 16 * MT), BF16, U1)
        self.acc = sb_at("v_acc", (128, NTT * C), F32, U1 + 16384)
        self.O_r = sb_at("v_Or", (128, 4 * MT), F32, U1 + 49152)
        self.a_r = sb_at("v_ar", (128, 4 * MT), F32, U1 + 57344)
        self.sp_r = sb_at("v_spr", (128, 4 * MT), F32, U1 + 65536)
        self.sg = [sb_at("v_sg0", (128, MT), F32, TR),
                   sb_at("v_sg1", (128, MT), F32, TR + 2048)]
        self.qgp = [sb_at(f"v_qgp{i}", (128, C), F32, U1 + i * 8192) for i in range(4)]
        self.qgn = [sb_at(f"v_qgn{i}", (128, C), F32, U1 + 32768 + i * 8192) for i in range(4)]
        self.redcols = sb_at("v_redcols", (128, 64), F32, U1 + 49152)
        self.cfp = [sb_at("v_cfp0", (128, C), F32, U1 + 16384),
                    sb_at("v_cfp1", (128, C), F32, U1 + 24576),
                    sb_at("v_cfp2", (128, C), F32, FM2)]
        self.cbf = [sb_at("v_cbf0", (128, C), BF16, U1 + 65536),
                    sb_at("v_cbf1", (128, C), BF16, U1 + 69632),
                    sb_at("v_cbf2", (128, C), BF16, FM2 + 8192),
                    sb_at("v_cbf3", (128, C), BF16, FM2 + 12288)]
        self.fm2 = sb_at("v_fm2", (128, KC * MT), BF16, FM2)
        _xt = sb_at("v_xt0", (128, C), F32, XT)
        self.xt = [_xt, _xt]
        self.trash = sb_at("v_trash", (128, C), F32, TR)
        self.lamrow = sb_at("v_lamrow", (1, H * 128), F32, TR)
        self.nwrow_s = sb_at("v_nwrow", (1, C), F32, TR)
        self.wa = [sb_at(f"v_wa{i}", (128, C), BF16, WA + i * 4096) for i in range(4)]
        self.waf = [sb_at("v_waf0", (128, C), F32, WA), sb_at("v_waf1", (128, C), F32, WA + 8192)]
        self.wbf = [sb_at("v_wbf0", (128, C), F32, WB), sb_at("v_wbf1", (128, C), F32, WB + 8192)]
        self.effx = [sb_at("v_effx0", (128, C), BF16, AT), sb_at("v_effx1", (128, C), BF16, AT + 4096)]
        self.wb = sb_at("v_wb", (128, 2 * I), BF16, WB)
        self.wbq = [sb_at(f"v_wbq{i}", (128, 4096), BF16, WB + i * 8192) for i in range(4)]
        self.smb = [sb_at("v_smb0", (128, C), BF16, SMB), sb_at("v_smb1", (128, C), BF16, SMB + 4096)]
        self.atts = {}
        for j, nm in enumerate(("sT", "ktr", "vtr", "qlam")):
            self.atts[nm] = [sb_at(f"v_{nm}0", (128, 512), BF16, AT + j * 2048),
                             sb_at(f"v_{nm}1", (128, 512), BF16, AT + j * 2048 + 1024)]

        self.pf = [nc.alloc_psum_tensor(f"pf{i}", [128, 512], F32) for i in range(6)]
        self.pt = [nc.alloc_psum_tensor(f"pt{i}", [128, 1024], BF16) for i in range(2)]
        nc.psum_base = 6
        self.pfd = [nc.alloc_psum_tensor(f"pfd{i}", [128, 512], F32) for i in range(2)]
        self.ptf = self.pfd[0]

        self.sCONST = _Sem(nc.alloc_semaphore("sCONST"))
        self.sBNC = _Sem(nc.alloc_semaphore("sBNC"))
        self.sWA = [_Sem(nc.alloc_semaphore(f"sWA{i}")) for i in range(4)]
        self.sWB = [_Sem(nc.alloc_semaphore(f"sWB{i}")) for i in range(4)]
        self.sXT = _Sem(nc.alloc_semaphore("sXT"))
        self.sSMB = [_Sem(nc.alloc_semaphore(f"sSMB{i}")) for i in range(2)]
        self.sDRAW = _Sem(nc.alloc_semaphore("sDRAW"))
        self.sDBF = _Sem(nc.alloc_semaphore("sDBF"))
        self.sCLD = [_Sem(nc.alloc_semaphore(f"sCLD{i}")) for i in range(4)]
        self.sCST = [_Sem(nc.alloc_semaphore(f"sCST{i}")) for i in range(4)]
        self.sX2 = _Sem(nc.alloc_semaphore("sX2"))
        self.sOUT = _Sem(nc.alloc_semaphore("sOUT"))
        self.sDBG = _Sem(nc.alloc_semaphore("sDBG"))
        self.sP = _Sem(nc.alloc_semaphore("sP"))
        self.sV = _Sem(nc.alloc_semaphore("sV"))
        self.sA = _Sem(nc.alloc_semaphore("sA"))
        self.sO = _Sem(nc.alloc_semaphore("sO"))
        self.all_dma_sems = ([self.sCONST, self.sBNC, self.sXT, self.sDRAW, self.sDBF,
                              self.sX2, self.sOUT, self.sDBG] + self.sWA + self.sWB +
                             self.sSMB + self.sCLD + self.sCST)
        self.all_sems = self.all_dma_sems + [self.sP, self.sV, self.sA, self.sO]

        self.bguard = {}
        self.rguard = {}
        self.dbg_tensors = {}

    def _app(self, eng, f):
        self.st[eng].append(f)

    def ins(self, eng, builder, waits=(), sem=None, inc=1):
        waits = tuple((s.h, v) for (s, v) in waits if v > 0)
        if sem is not None:
            sem.n += inc
        def f(e, b=builder, waits=waits, sh=(sem.h if sem else None), inc=inc):
            for hh, v in waits:
                e.wait_ge(hh, v)
            i = b(e)
            if sh is not None:
                i.then_inc(sh, inc)
        self._app(eng, f)
        return sem.n if sem else None

    @staticmethod
    def bc(col, n):
        return col.broadcast_to([128, n])

    def pad(self, eng, n=10):
        for _ in range(n):
            self.ins(eng, (lambda e: e.engine_nop()))

    def vfence(self):
        t = self.sV.tag()
        self.ins("vector", (lambda e: e.engine_nop()), waits=[t], sem=self.sV)

    def dma(self, q, out, in_, sem=None, waits=()):
        self.ins(q, (lambda e, o=out, ii=in_: e.dma_start(out=o, in_=ii)),
                 waits=waits, sem=sem, inc=16)
        return (sem, sem.n)

    def dmas(self, q, out, in_, sem=None, waits=(), nsplit=1):
        if nsplit == 1:
            return self.dma(q, out, in_, sem=sem, waits=waits)
        ncol = out.shape[-1]
        w = ncol // nsplit
        for k in range(nsplit):
            self.ins(q, (lambda e, o=out[:, k * w:(k + 1) * w],
                         ii=in_[:, k * w:(k + 1) * w]: e.dma_start(out=o, in_=ii)),
                     waits=(waits if k == 0 else ()), sem=sem, inc=16)
        return (sem, sem.n)

    def barrier(self, engines=("tensor", "vector", "scalar", "gpsimd")):
        tags = [self.sP.tag(), self.sV.tag(), self.sA.tag(), self.sO.tag()]
        tags = [(s.h, v) for (s, v) in tags if v > 0]
        for eng in engines:
            def f(e, tags=tuple(tags)):
                for hh, v in tags:
                    e.wait_ge(hh, v)
            self._app(eng, f)

    def mm(self, out, lhsT, rhs, start, stop, waits=(), sem=None, inc=1):
        return self.ins("tensor", (lambda e, o=out, l=lhsT, r=rhs, st=start, sp=stop:
                                   e.matmul(o, l, r, start=st, stop=sp)),
                        waits=waits, sem=sem, inc=inc)

    def tr(self, out, in_, ident=None, waits=(), sem=None, inc=1):
        idt = (ident if ident is not None else self.identb[:])
        return self.ins("tensor", (lambda e, o=out, ii=in_, idt=idt: e.transpose(o, ii, idt)),
                        waits=waits, sem=sem, inc=inc)

    def cpy(self, eng, out, in_, waits=(), sem=None, inc=1):
        return self.ins(eng, (lambda e, o=out, ii=in_: e.tensor_copy(o, ii)),
                        waits=waits, sem=sem, inc=inc)

    def tt(self, eng, out, a, b, op, waits=(), sem=None, inc=1):
        return self.ins(eng, (lambda e, o=out, x=a, y=b, op=op: e.tensor_tensor(o, x, y, op)),
                        waits=waits, sem=sem, inc=inc)

    def ts(self, eng, out, a, s1, op0, s2=None, op1=None, waits=(), sem=None, inc=1):
        if s2 is None:
            return self.ins(eng, (lambda e, o=out, x=a, s1=s1, op0=op0:
                                  e.tensor_single_scalar(o, x, s1, op0)),
                            waits=waits, sem=sem, inc=inc)
        return self.ins(eng, (lambda e, o=out, x=a, s1=s1, s2=s2, op0=op0, op1=op1:
                              e.tensor_scalar(o, x, s1, s2, op0, op1)),
                        waits=waits, sem=sem, inc=inc)

    def stt(self, eng, out, in0, scalar, in1, op0, op1, accum=None, waits=(), sem=None, inc=1):
        return self.ins(eng, (lambda e, o=out, a=in0, s=scalar, b=in1, op0=op0, op1=op1, ac=accum:
                              e.scalar_tensor_tensor(o, a, s, b, op0, op1, accum_out=ac)),
                        waits=waits, sem=sem, inc=inc)

    def act(self, out, in_, func, bias=0.0, scale=1.0, accum=None, waits=(), sem=None, inc=1):
        return self.ins("scalar", (lambda e, o=out, ii=in_, fn=func, b=bias, s=scale, ac=accum:
                                   e.activation(o, ii, fn, bias=b, scale=s, accum_out=ac)),
                        waits=waits, sem=sem, inc=inc)

    def acopy(self, out, in_, waits=(), sem=None, inc=1):
        return self.ins("scalar", (lambda e, o=out, ii=in_: e.copy(o, ii)),
                        waits=waits, sem=sem, inc=inc)

    def red(self, eng, out, in_, op=ALU.add, absval=False, waits=(), sem=None, inc=1):
        return self.ins(eng, (lambda e, o=out, ii=in_, op=op, av=absval:
                              e.tensor_reduce(o, ii, AX.X, op, apply_absolute_value=av)),
                        waits=waits, sem=sem, inc=inc)

    def memset(self, eng, ap, v, waits=(), sem=None, inc=1):
        return self.ins(eng, (lambda e, a=ap, v=v: e.memset(a, v)),
                        waits=waits, sem=sem, inc=inc)

    def dump(self, name, ap, shape, dt=F32):
        if name not in self.dbg:
            return
        t = self.nc.dram_tensor("dbg_" + name, list(shape), dt, kind="ExternalOutput").ap()
        self.dbg_tensors[name] = t
        waits = [t2 for t2 in (self.sV.tag(), self.sA.tag(), self.sP.tag(), self.sO.tag())]
        wt = self.dma("scalar", t, ap, sem=self.sDBG, waits=waits)
        for eng in ("tensor", "vector", "scalar", "gpsimd", "sync"):
            self.ins(eng, (lambda e: e.nop()), waits=[wt])

    def bw(self, *banks):
        out = []
        for b in banks:
            g = self.bguard.get(b)
            if g:
                out.append(g)
        return out

    def bset(self, bank, tag):
        self.bguard[bank] = tag

    def rw(self, name):
        g = self.rguard.get(name)
        return [g] if g else []

    def rset(self, name, tag):
        self.rguard[name] = tag

    def slw(self, slots):
        out = []
        for s in slots:
            g = self.rguard.get(s)
            if g:
                out.append(g)
        return out

    def slset(self, slots, tag):
        for s in slots:
            self.rguard[s] = tag

    def emit_setup(self):
        d = self.dr
        last = None
        for nm, dst in (("identb", self.identb), ("identf", self.identf),
                        ("diffm", self.diffm), ("tril", self.tril),
                        ("qnw", self.qnw_s), ("nw16", self.nw_s), ("pnw16", self.pnw_s),
                        ("rampc", self.rampc_s), ("ramprc", self.ramprc_s),
                        ("rampc16", self.rampc16_s), ("ramprc16", self.ramprc16_s),
                        ("onescf", self.onescf_s), ("ones1rf", self.ones1rf_s),
                        ("srow", self.srow_s), ("fconsts", self.fconsts)):
            last = self.dma("sync", dst[:], d[nm], sem=self.sCONST)
        wld = [last]

        self.mm(self.ptf[:, 0:16], self.ones1rf_s[:], self.srow_s[:], True, True,
                waits=wld, sem=self.sP)
        self.cpy("vector", self.s_bc[:], self.ptf[:, 0:16], waits=[self.sP.tag()], sem=self.sV)
        self.bset(6, self.sV.tag())
        self.vfence()
        self.ts("vector", self.nsbc[:], self.s_bc[:], -1.0, ALU.mult, sem=self.sV)
        self.ts("vector", self.rfq[:], self.s_bc[:], -float(T), ALU.mult, sem=self.sV)
        self.vfence()
        self.tt("vector", self.lam_t16[:], self.nsbc[:], self.rampc16_s[:], ALU.mult, sem=self.sV)
        self.tt("vector", self.lam_rev16[:], self.nsbc[:], self.ramprc16_s[:], ALU.mult, sem=self.sV)
        self.vfence()
        va = self.sV.tag()
        self.act(self.lam_t16[:], self.lam_t16[:], AF.Exp, waits=[va], sem=self.sA)
        self.act(self.lam_rev16[:], self.lam_rev16[:], AF.Exp, sem=self.sA)
        self.act(self.lamT16[:], self.rfq[:], AF.Exp, sem=self.sA)
        aa = self.sA.tag()

        d3 = self.diffm[:][:, None, :].broadcast_to([128, 16, 128])
        n3 = self.nsbc[:][:, :, None].broadcast_to([128, 16, 128])
        t3 = self.trash[:].rearrange("p (h t) -> p h t", t=128)
        self.tt("vector", t3, d3, n3, ALU.mult, sem=self.sV)
        self.act(self.masksT[:], self.trash[:], AF.Exp, waits=[self.sV.tag()], sem=self.sA)
        tr3 = self.tril[:][:, None, :].broadcast_to([128, 16, 128])
        m3 = self.masksT[:].rearrange("p (h t) -> p h t", t=128)
        self.tt("vector", m3, m3, tr3, ALU.mult, waits=[self.sA.tag()], sem=self.sV)

        for (src, bncT, dst) in ((self.lam_t16, self.bnc1, self.lam_t_bc),):
            self.tr(self.ptf[0:16, 0:128], src[:, 0:16], ident=self.identf[:],
                    waits=[self.sA.tag()] + self.bw(6), sem=self.sP)
            self.cpy("vector", self.trash[0:16, 0:128], self.ptf[0:16, 0:128],
                     waits=[self.sP.tag()], sem=self.sV)
            self.bset(6, self.sV.tag())
            wb1 = self.dma("sync", bncT, self.trash[0:16, 0:128], sem=self.sBNC,
                           waits=[self.sV.tag()])
            wrow = self.dma("sync", self.lamrow[:],
                            bncT.rearrange("h t -> (h t)")[None, :],
                            sem=self.sBNC, waits=[wb1])
            for j in range(4):
                self.mm(self.pf[j][:], self.ones1rf_s[:], self.lamrow[:, j * 512:(j + 1) * 512],
                        True, True, waits=([wrow] + self.bw(j)), sem=self.sP)
            mt = self.sP.tag()
            for j in range(4):
                self.cpy("vector", dst[:, j * 512:(j + 1) * 512], self.pf[j][:],
                         waits=[mt] if j == 0 else (), sem=self.sV)
                self.bset(j, self.sV.tag())
            self.rset("lamrow", self.sV.tag())

        nwl = self.dma("sync", self.nwrow_s[:], d["nwrow"], sem=self.sCONST,
                       waits=self.rw("lamrow"))
        for j in range(4):
            self.mm(self.pf[j][:], self.ones1rf_s[:], self.nwrow_s[:, j * 512:(j + 1) * 512],
                    True, True, waits=([nwl] + self.bw(j)), sem=self.sP)
        mt = self.sP.tag()
        for j in range(4):
            self.cpy("vector", self.nw2_bc[:, j * 512:(j + 1) * 512], self.pf[j][:],
                     waits=[mt] if j == 0 else (), sem=self.sV)
            self.bset(j, self.sV.tag())
        self.tt("vector", self.nw2_bc[:], self.nw2_bc[:], self.nw2_bc[:], ALU.mult, sem=self.sV)

        self.memset("vector", self.S[:], 0.0, sem=self.sV)
        self.memset("vector", self.Sbf[0][:], 0.0, sem=self.sV)
        self.memset("vector", self.accq[:], 0.0, sem=self.sV)
        self.memset("vector", self.accp[:], 0.0, sem=self.sV)
        self.dump("accq0", self.accq[:], (128, 1))
        self.dump("smalls0", self.stn[0][:], (128, 1))
        self.dump("masksT", self.masksT[:], (128, H * 128), BF16)
        self.dump("lam_t_bc", self.lam_t_bc[:], (128, H * 128), BF16)

    def emit_quant(self):
        d = self.dr
        qslab = [self.waf[0], self.waf[1], self.wbf[0], self.wbf[1]]
        for grp, n, base in (("qkvT", 32, 0), ("vT", KC, 32), ("pT", KC, 48)):
            for i in range(n):
                j = (i + base) % 4
                wslab = qslab[j]
                wt = self.dmas(("sync", "scalar")[j % 2], wslab[:], d[grp][i],
                               sem=self.sWA[j], waits=self.slw((f"QW{j}",)))
                col = base + i
                self.red("vector", self.redcols[:, col:col + 1], wslab[:], absval=True,
                         waits=[wt], sem=self.sV)
                self.slset((f"QW{j}",), self.sV.tag())
        self.red("vector", self.accq[:], self.redcols[:, 0:48], sem=self.sV)
        self.red("vector", self.accp[:], self.redcols[:, 48:64], sem=self.sV)
        self.vfence()
        self.dump("redcols", self.redcols[:], (128, 64))
        self.dump("accq", self.accq[:], (128, 1))
        self.dump("accp", self.accp[:], (128, 1))
        for acc, mcol, nelem in ((self.accq, self.mq, 6144 * 2048),
                                 (self.accp, self.mp, 2048 * 2048)):
            self.mm(self.ptf[0:1, 0:1], acc[:], self.onescf_s[:], True, True,
                    waits=[self.sV.tag()] + self.bw(6), sem=self.sP)
            self.cpy("vector", self.stot[:], self.ptf[0:1, 0:1], waits=[self.sP.tag()], sem=self.sV)
            self.vfence()
            self.mm(self.ptf[:, 4:5], self.ones1rf_s[:], self.stot[:], True, True,
                    waits=[self.sV.tag()], sem=self.sP)
            self.ts("vector", mcol[:], self.ptf[:, 4:5], 1.0 / nelem, ALU.mult, 1e-5, ALU.max,
                    waits=[self.sP.tag()], sem=self.sV)
            self.vfence()
            self.bset(6, self.sV.tag())
        self.ts("vector", self.thrq[:], self.mq[:], 0.5, ALU.mult, sem=self.sV)
        self.ts("vector", self.nthrq[:], self.mq[:], -0.5, ALU.mult, sem=self.sV)
        self.ts("vector", self.thrp[:], self.mp[:], 0.5, ALU.mult, sem=self.sV)
        self.ts("vector", self.nthrp[:], self.mp[:], -0.5, ALU.mult, sem=self.sV)
        self.tt("vector", self.rfp[:], self.nw_s[:], self.pnw_s[:], ALU.mult, sem=self.sV)
        self.tt("vector", self.rfq[:], self.qnw_s[:], self.bc(self.mq[:, 0:1], 16),
                ALU.mult, sem=self.sV)
        self.vfence()
        self.ts("vector", self.rfq2[:], self.rfq[:], SCALE, ALU.mult, sem=self.sV)
        self.tt("vector", self.rfp[:], self.rfp[:], self.bc(self.mp[:, 0:1], 16),
                ALU.mult, sem=self.sV)
        self.vfence()
        thr_tag = self.sV.tag()
        self.dump("mq", self.mq[:], (128, 1))
        self.dump("mp", self.mp[:], (128, 1))

        effs = [self.smb[0], self.smb[1], self.effx[0], self.effx[1]]
        effg = ["smb0", "smb1", "eff2", "eff3"]
        effsem = [self.sSMB[0], self.sSMB[1], self.sCST[0], self.sCST[1]]
        last_eff = [None] * 4
        for grp, n, base, cache, thr, nthr, rf, kind in (
                ("qkvT", 32, 0, self.cqkv, self.thrq, self.nthrq, self.rfq, "qk"),
                ("vT", KC, 32, self.cv, self.thrq, self.nthrq, self.rfq, "v"),
                ("pT", KC, 48, self.cpw, self.thrp, self.nthrp, self.rfp, "p")):
            for i in range(n):
                ii = i + base
                j = ii % 4
                par = ii % 4
                wslab = qslab[j]
                wt = self.dmas(("sync", "scalar")[j % 2], wslab[:], d[grp][i],
                               sem=self.sWA[j], waits=self.slw((f"QW{j}", f"QWv{j}")))
                qgp, qgn = self.qgp[par], self.qgn[par]
                self.ts("vector", qgp[:], wslab[:], thr[:, 0:1], ALU.is_gt,
                        waits=[wt] + self.rw(f"qg{par}"), sem=self.sV)
                self.ts("vector", qgn[:], wslab[:], nthr[:, 0:1], ALU.is_lt, sem=self.sV)
                self.rguard[f"QW{j}"] = self.sV.tag()
                self.tt("vector", qgp[:], qgp[:], qgn[:], ALU.subtract, sem=self.sV)
                eff = effs[j]
                if kind == "v":
                    self.ts("vector", eff[:], qgp[:], self.rfq[:, i:i + 1], ALU.mult,
                            waits=self.rw(effg[j]), sem=self.sV)
                else:
                    g3 = qgp[:].rearrange("p (kc n) -> p kc n", n=128)
                    e3 = eff[:].rearrange("p (kc n) -> p kc n", n=128)
                    rfu = self.rfq2 if (kind == "qk" and i < 16) else rf
                    r3 = rfu[:][:, :, None].broadcast_to([128, KC, 128])
                    self.tt("vector", e3, g3, r3, ALU.mult,
                            waits=self.rw(effg[j]), sem=self.sV)
                self.rset(f"qg{par}", self.sV.tag())
                last_eff[j] = self.sV.tag()
                wt2 = self.dmas("gpsimd", cache[i], eff[:], sem=effsem[j],
                                waits=[self.sV.tag()])
                self.rset(effg[j], wt2)
        qend = [self.sO.tag(), self.sV.tag()]
        self.rset("cfp0", qend[0])
        self.rset("cfp1", qend[0])
        self.rset("endD_P", qend[0])
        for s in ("WA0", "WA1"):
            self.rguard[s] = last_eff[0]
        for s in ("WA2", "WA3"):
            self.rguard[s] = last_eff[1]
        self.rguard["WB0"] = last_eff[2]
        self.rguard["WB1"] = last_eff[3]
        for s in ("sT0", "sT1", "ktr0", "ktr1"):
            self.rguard[s] = self.rguard["eff2"]
        for s in ("vtr0", "vtr1", "qlam0", "qlam1"):
            self.rguard[s] = self.rguard["eff3"]
        self.cast_tasks = []
        self.cast_i = 0
        self.cast_done_tags = []
        for grp, cache in (("gT", self.cgw), ("uT", self.cuw)):
            for i in range(64):
                self.cast_tasks.append((d[grp][i], cache[i]))
        for i in range(32):
            for hh in range(2):
                self.cast_tasks.append((d["dT"][i][:, hh * C:(hh + 1) * C],
                                        self.cdw[i][:, hh * C:(hh + 1) * C]))
        self.quant_done_W = [self.sSMB[0].tag(), self.sSMB[1].tag(),
                             self.sCST[0].tag(), self.sCST[1].tag()]
        if "cqkv" in self.dbg:
            t = self.nc.dram_tensor("dbg_cqkv", [48, 128, C], BF16, kind="ExternalOutput").ap()
            self.dbg_tensors["cqkv"] = t
            for i in range(48):
                self.dma("scalar", t[i], self.cqkv[i], sem=self.sDBG,
                         waits=self.quant_done_W)
        if "cdw" in self.dbg:
            t = self.nc.dram_tensor("dbg_cdw", [32, 128, 4096], BF16, kind="ExternalOutput").ap()
            self.dbg_tensors["cdw"] = t
            for i in range(32):
                self.dma("scalar", t[i], self.cdw[i], sem=self.sDBG,
                         waits=self.quant_done_W)

    def _emit_casts(self, n):
        for _ in range(min(n, len(self.cast_tasks))):
            srcap, dstap = self.cast_tasks.pop(0)
            i = self.cast_i
            self.cast_i += 1
            p4, p2 = i % 3, i % 4
            wt = self.dmas(("sync", "scalar")[i % 2], self.cfp[p4][:], srcap,
                           sem=self.sCLD[p4], waits=self.rw(f"cfp{p4}"))
            ceng = ("vector", "gpsimd", "scalar")[i % 3]
            sem = {"vector": self.sV, "gpsimd": self.sO, "scalar": self.sA}[ceng]
            if ceng == "scalar":
                self.acopy(self.cbf[p2][:], self.cfp[p4][:],
                           waits=[wt] + self.rw(f"cbf{p2}"), sem=sem)
            else:
                self.cpy(ceng, self.cbf[p2][:], self.cfp[p4][:],
                         waits=[wt] + self.rw(f"cbf{p2}"), sem=sem)
            ctag = sem.tag()
            self.rset(f"cfp{p4}", ctag)
            wt2 = self.dmas("gpsimd" if ceng == "vector" else ceng, dstap,
                            self.cbf[p2][:], sem=self.sCST[p2], waits=[ctag])
            self.rset(f"cbf{p2}", wt2)
        if not self.cast_tasks:
            self.cast_done_tags = [s.tag() for s in self.sCST]

    def _ln_coeffs(self, xa, eps, extra_rms=None):
        st = self.stn
        self.red("vector", st[0][:], xa, sem=self.sV)
        self.stt("vector", self.trash[:], xa, 1.0, xa, ALU.bypass, ALU.mult,
                 accum=st[1][:, 0:1], sem=self.sV)
        self.vfence()
        self.ts("vector", st[2][:], st[0][:], 1.0 / C, ALU.mult, sem=self.sV)
        self.ts("vector", st[3][:], st[1][:], 1.0 / C, ALU.mult, sem=self.sV)
        self.vfence()
        self.tt("vector", st[4][:], st[2][:], st[2][:], ALU.mult, sem=self.sV)
        self.vfence()
        self.tt("vector", st[5][:], st[3][:], st[4][:], ALU.subtract, sem=self.sV)
        self.vfence()
        self.ts("vector", st[6][:], st[5][:], eps, ALU.add, sem=self.sV)
        self.vfence()
        vv = self.sV.tag()
        self.act(st[7][:], st[6][:], AF.Ln, waits=[vv], sem=self.sA)
        self.ts("vector", st[6][:], st[7][:], -0.5, ALU.mult,
                waits=[self.sA.tag()], sem=self.sV)
        self.vfence()
        self.act(st[7][:], st[6][:], AF.Exp, waits=[self.sV.tag()], sem=self.sA)
        aa = self.sA.tag()
        if extra_rms is None:
            self.stt("vector", st[8][:], st[2][:], -1.0, st[7][:], ALU.mult, ALU.mult,
                     waits=[aa], sem=self.sV)
            self.vfence()
            return st[7], st[8]
        self.tt("vector", st[8][:], st[7][:], st[7][:], ALU.mult, waits=[aa], sem=self.sV)
        self.vfence()
        self.tt("vector", st[9][:], st[5][:], st[8][:], ALU.mult, sem=self.sV)
        self.vfence()
        self.ts("vector", st[9][:], st[9][:], extra_rms, ALU.add, sem=self.sV)
        self.vfence()
        vv = self.sV.tag()
        self.act(st[10][:], st[9][:], AF.Ln, waits=[vv], sem=self.sA)
        self.ts("vector", st[9][:], st[10][:], -0.5, ALU.mult,
                waits=[self.sA.tag()], sem=self.sV)
        self.vfence()
        self.act(st[10][:], st[9][:], AF.Exp, waits=[self.sV.tag()], sem=self.sA)
        aa = self.sA.tag()
        self.tt("vector", st[11][:], st[7][:], st[10][:], ALU.mult, waits=[aa], sem=self.sV)
        self.vfence()
        self.stt("vector", st[8][:], st[2][:], -1.0, st[11][:], ALU.mult, ALU.mult,
                 sem=self.sV)
        self.vfence()
        return st[11], st[8]

    def _transpose_to_fm(self, src_tm, dst_fm, tt, tag_wait, extra_evac_waits=()):
        for half in range(2):
            bank = 6 + half
            bwt = self.bw(bank)
            for j in range(8):
                kc = half * 8 + j
                self.tr(self.pt[half][:, j * 128:(j + 1) * 128],
                        src_tm[:, kc * 128:(kc + 1) * 128],
                        waits=([tag_wait] + bwt) if j == 0 else (),
                        sem=self.sP if j == 7 else None)
            mt = self.sP.tag()
            dst = dst_fm[:].rearrange("p (kc t) -> p kc t", t=MT)[:, half * 8:(half + 1) * 8,
                                                                 tt * 128:(tt + 1) * 128]
            self.cpy("vector", dst, self.pt[half][:],
                     waits=[mt] + list(extra_evac_waits), sem=self.sV)
            self.bset(bank, self.sV.tag())

    def emit_phaseA(self, sc, prologue):
        d = self.dr
        xsrc = d["xprev"] if prologue else d["x"]
        for tt in range(NTT):
            par = tt % 2
            row0 = sc * MT + tt * 128
            wt = self.dmas("scalar", self.xt[par][:], xsrc[row0:row0 + 128, :], sem=self.sXT,
                           waits=self.rw("xtg"))
            self.ins("vector", (lambda e: e.engine_nop()), waits=[wt])
            r_ap, b_ap = self._ln_coeffs(self.xt[par][:], 1e-5, extra_rms=EPS32)
            self.tt("vector", self.trash[:], self.xt[par][:],
                    self.bc(r_ap[:, 0:1], C), ALU.mult, sem=self.sV)
            self.tt("vector", self.smb[par][:], self.trash[:],
                    self.bc(b_ap[:, 0:1], C), ALU.add,
                    waits=self.rw(f"smb{par}"), sem=self.sV)
            self.rset("xtg", self.sV.tag())
            self._transpose_to_fm(self.smb[par][:], self.h_fm, tt, self.sV.tag(),
                                  extra_evac_waits=self.rw("endD_P"))
            self.rset(f"smb{par}", self.sV.tag())
        h_ready = self.sV.tag()

        firstv = True
        for half in range(2):
            for kc in range(KC):
                par = kc % 4
                slab = self.wa[par]
                wt = self.dmas(("sync", "scalar")[par % 2], slab[:], self.cv[kc],
                               sem=self.sWA[par],
                               waits=self.slw((f"WA{par}",)) +
                                     (self.quant_done_W
                                      if (prologue and half == 0 and kc < 2) else []))
                firstv = False
                for t8 in range(8):
                    tt = half * 2 + t8 // 4
                    hg = t8 % 4
                    dst = self.pf[t8][:] if t8 < 6 else self.pfd[t8 - 6][:]
                    w = []
                    if t8 == 0:
                        w += [wt, h_ready] if kc == 0 else [wt]
                    if kc == 0:
                        w += self.bw(t8)
                    self.mm(dst, self.h_fm[:, kc * MT + tt * 128:kc * MT + (tt + 1) * 128],
                            slab[:, hg * 512:(hg + 1) * 512],
                            kc == 0, kc == KC - 1,
                            waits=w, sem=self.sP if t8 == 7 else None)
                self.slset((f"WA{par}",), self.sP.tag())
            vt = self.sP.tag()
            for t8 in range(8):
                tt = half * 2 + t8 // 4
                hg = t8 % 4
                src = self.pf[t8][:] if t8 < 6 else self.pfd[t8 - 6][:]
                self.cpy("vector", self.v_tm[:, (tt * 4 + hg) * 512:(tt * 4 + hg + 1) * 512],
                         src, waits=[vt] if t8 == 0 else (), sem=self.sV)
                self.bset(t8, self.sV.tag())

        rng_ncn = range(16, 32) if prologue else range(32)
        dsts = {0: self.q_fm, 1: self.k_fm}
        first = True
        for ncn in rng_ncn:
            par = ncn % 4
            slab = self.wa[par]
            wt = self.dmas(("sync", "scalar")[par % 2], slab[:], self.cqkv[ncn],
                           sem=self.sWA[par], waits=self.slw((f"WA{par}",)))
            bank = ncn % 3
            for kc in range(KC):
                self.mm(self.pf[bank][:], slab[:, kc * 128:(kc + 1) * 128],
                        self.h_fm[:, kc * MT:kc * MT + MT],
                        kc == 0, kc == KC - 1,
                        waits=([wt] + self.bw(bank)) if kc == 0 else (),
                        sem=self.sP if kc == KC - 1 else None)
            self.slset((f"WA{par}",), self.sP.tag())
            hd = ncn % 16
            self.cpy("vector", dsts[ncn // 16][:, hd * MT:(hd + 1) * MT], self.pf[bank][:],
                     waits=[self.sP.tag()] +
                           ((self.rw("accw") + self.rw("d_act_done")) if first else []) +
                           (self.cast_done_tags
                            if (first and not prologue and sc == 0) else []),
                     sem=self.sV)
            self.bset(bank, self.sV.tag())
            first = False
        self.qkv_ready = self.sV.tag()
        self.rset("hfm_free", self.sP.tag())
        if sc == 0 and not prologue:
            self.dump("h_fm", self.h_fm[:], (128, KC * MT), BF16)
            self.dump("q_fm", self.q_fm[:], (128, H * MT), BF16)
            self.dump("k_fm", self.k_fm[:], (128, H * MT), BF16)
            self.dump("v_tm", self.v_tm[:], (128, H * MT), BF16)

    def emit_phaseB(self, sc, prologue, gchunk0):
        sbf_tag = [None] * NCH
        for ch in range(NCH + (0 if prologue else 1)):
            if ch < NCH:
                sslot = ch % 2
                if not prologue:
                    self.cpy("vector", self.Sbf[sslot][:], self.S[:],
                             waits=self.rw(f"Sbf{sslot}"), sem=self.sV)
                    sbf_tag[ch] = self.sV.tag()
                for g in range(4):
                    hp = g % 2
                    h0 = 4 * g
                    bank6 = 6 + hp
                    bwt = self.bw(bank6)
                    for j in range(4):
                        h = h0 + j
                        self.tr(self.pt[hp][:, j * 128:(j + 1) * 128],
                                self.k_fm[:, h * MT + ch * 128:h * MT + (ch + 1) * 128],
                                waits=(bwt + [self.qkv_ready]) if j == 0 else (),
                                sem=self.sP if j == 3 else None)
                    trt = self.sP.tag()
                    ktr3 = self.atts["ktr"][hp][:].rearrange("p (h d) -> p h d", d=128)
                    pt3 = self.pt[hp][:, 0:512].rearrange("p (h d) -> p h d", d=128)
                    lr3 = self.lam_rev16[:, h0:h0 + 4][:, :, None].broadcast_to([128, 4, 128])
                    self.tt("vector", ktr3, pt3, lr3, ALU.mult,
                            waits=[trt] + self.rw(f"ktr{hp}"), sem=self.sV)
                    self.bset(bank6, self.sV.tag())
                    ktag = self.sV.tag()
                    kbank = 4 + hp
                    vbase = (ch * 4 + g) * 512
                    for j in range(4):
                        self.mm(self.pf[kbank][:, j * 128:(j + 1) * 128],
                                self.atts["ktr"][hp][:, j * 128:(j + 1) * 128],
                                self.v_tm[:, vbase + j * 128:vbase + (j + 1) * 128],
                                True, True,
                                waits=([ktag] + self.bw(kbank)) if j == 0 else (),
                                sem=self.sP if j == 3 else None)
                    kt = self.sP.tag()
                    self.rset(f"ktr{hp}", kt)
                    for j in range(4):
                        h = h0 + j
                        self.stt("vector", self.S[:, h * 128:(h + 1) * 128],
                                 self.S[:, h * 128:(h + 1) * 128], self.lamT16[:, h:h + 1],
                                 self.pf[kbank][:, j * 128:(j + 1) * 128], ALU.mult, ALU.add,
                                 waits=[kt] if j == 0 else (),
                                 sem=self.sV if j == 3 else None)
                    self.bset(kbank, self.sV.tag())
            if prologue or ch == 0:
                continue
            oc = ch - 1
            sslot = oc % 2
            o_par = oc % 2
            for g in range(4):
                hp = g % 2
                h0 = 4 * g
                for j in range(4):
                    h = h0 + j
                    self.mm(self.pf[hp][:, j * 128:(j + 1) * 128],
                            self.k_fm[:, h * MT + oc * 128:h * MT + (oc + 1) * 128],
                            self.q_fm[:, h * MT + oc * 128:h * MT + (oc + 1) * 128],
                            True, True,
                            waits=self.bw(hp) if j == 0 else (),
                            sem=self.sP if j == 3 else None)
                self.tt("vector", self.atts["sT"][hp][:], self.pf[hp][:],
                        self.masksT[:, h0 * 128:(h0 + 4) * 128], ALU.mult,
                        waits=[self.sP.tag()] + self.rw(f"sT{hp}"), sem=self.sV)
                self.bset(hp, self.sV.tag())
                qsl = self.q_fm[:].rearrange("p (h t) -> p h t", t=MT)[:, h0:h0 + 4,
                                                                      oc * 128:(oc + 1) * 128]
                lam4 = self.lam_t_bc[:].rearrange("p (h t) -> p h t", t=128)[:, h0:h0 + 4, :]
                qlam4 = self.atts["qlam"][hp][:].rearrange("p (h t) -> p h t", t=128)
                self.tt("vector", qlam4, qsl, lam4, ALU.mult,
                        waits=self.rw(f"qlam{hp}"), sem=self.sV)
                vready = self.sV.tag()
                obank = 2 + hp
                vbase = (oc * 4 + g) * 512
                for j in range(4):
                    h = h0 + j
                    self.mm(self.pf[obank][:, j * 128:(j + 1) * 128],
                            self.atts["sT"][hp][:, j * 128:(j + 1) * 128],
                            self.v_tm[:, vbase + j * 128:vbase + (j + 1) * 128],
                            True, False,
                            waits=([vready, sbf_tag[oc]] + self.bw(obank))
                                  if j == 0 else ())
                    self.mm(self.pf[obank][:, j * 128:(j + 1) * 128],
                            self.atts["qlam"][hp][:, j * 128:(j + 1) * 128],
                            self.Sbf[sslot][:, h * 128:(h + 1) * 128],
                            False, True, sem=self.sP if j == 3 else None)
                ot = self.sP.tag()
                self.rset(f"sT{hp}", ot)
                self.rset(f"qlam{hp}", ot)
                if g == 3:
                    self.rset(f"Sbf{sslot}", ot)
                self.acopy(self.o_sb[o_par][:, h0 * 128:(h0 + 4) * 128],
                           self.pf[obank][:],
                           waits=[ot] + ((self.rw(f"osb{o_par}") + self.rw("hfm_free"))
                                         if g == 0 else []),
                           sem=self.sA)
                self.bset(obank, self.sA.tag())

            if True:
                oa = self.o_sb[o_par][:]
                st = self.stn
                self.stt("vector", self.trash[:], oa, 1.0, oa, ALU.bypass, ALU.mult,
                         accum=st[0][:, 0:1], waits=[self.sA.tag()], sem=self.sV)
                self.tt("vector", self.trash[:], oa, self.nw2_bc[:], ALU.mult, sem=self.sV)
                self.stt("vector", self.trash[:], self.trash[:], 1.0, oa, ALU.bypass, ALU.mult,
                         accum=st[1][:, 0:1], sem=self.sV)
                self.vfence()
                self.ts("vector", st[2][:], st[0][:], 1.0 / C, ALU.mult, 1e-6, ALU.add,
                        sem=self.sV)
                self.ts("vector", st[3][:], st[1][:], 1.0 / C, ALU.mult, sem=self.sV)
                self.vfence()
                vv = self.sV.tag()
                self.act(st[4][:], st[2][:], AF.Ln, waits=[vv], sem=self.sA)
                self.ts("vector", st[5][:], st[4][:], -0.5, ALU.mult,
                        waits=[self.sA.tag()], sem=self.sV)
                self.vfence()
                self.act(st[4][:], st[5][:], AF.Exp, waits=[self.sV.tag()], sem=self.sA)
                self.tt("vector", st[5][:], st[4][:], st[4][:], ALU.mult,
                        waits=[self.sA.tag()], sem=self.sV)
                self.vfence()
                self.tt("vector", st[6][:], st[3][:], st[5][:], ALU.mult, sem=self.sV)
                self.vfence()
                self.ts("vector", st[6][:], st[6][:], EPS32, ALU.add, sem=self.sV)
                self.vfence()
                vv = self.sV.tag()
                self.act(st[7][:], st[6][:], AF.Ln, waits=[vv], sem=self.sA)
                self.ts("vector", st[6][:], st[7][:], -0.5, ALU.mult,
                        waits=[self.sA.tag()], sem=self.sV)
                self.vfence()
                self.act(st[7][:], st[6][:], AF.Exp, waits=[self.sV.tag()], sem=self.sA)
                self.tt("vector", st[8][:], st[4][:], st[7][:], ALU.mult,
                        waits=[self.sA.tag()], sem=self.sV)
                self.vfence()
                par_s = oc % 2
                self.tt("vector", self.smb[par_s][:], oa, self.bc(st[8][:, 0:1], C),
                        ALU.mult, waits=self.rw(f"smb{par_s}"), sem=self.sV)
                self.rset(f"osb{o_par}", self.sV.tag())
                self._transpose_to_fm(self.smb[par_s][:], self.fm2, oc, self.sV.tag(),
                                      extra_evac_waits=self.rw("endD_P"))
                self.rset(f"smb{par_s}", self.sV.tag())
        if not prologue:
            self.out2_ready = self.sV.tag()
            self.rset("endB_P", self.sP.tag())
            if sc == 0:
                self.dump("o_sc0", self.o_sb[1][:], (128, C))
                self.dump("out2_fm", self.fm2[:], (128, KC * MT), BF16)
                self.dump("S_sc0", self.S[:], (128, H * 128))
        elif sc == self.n_prol - 1:
            self.dump("S_init", self.S[:], (128, H * 128))

    def emit_phaseC(self, sc):
        d = self.dr
        for ncn in range(16):
            par = ncn % 4
            slab = self.wa[par]
            wt = self.dmas(("sync", "scalar")[par % 2], slab[:], self.cpw[ncn],
                           sem=self.sWA[par], waits=self.slw((f"WA{par}",)))
            bank = ncn % 3
            for kc in range(KC):
                self.mm(self.pf[bank][:], slab[:, kc * 128:(kc + 1) * 128],
                        self.fm2[:, kc * MT:kc * MT + MT],
                        kc == 0, kc == KC - 1,
                        waits=([wt, self.out2_ready] + self.bw(bank))
                              if kc == 0 else (),
                        sem=self.sP if kc == KC - 1 else None)
            self.slset((f"WA{par}",), self.sP.tag())
            self.cpy("vector", self.attn_fm[:, ncn * MT:(ncn + 1) * MT], self.pf[bank][:],
                     waits=[self.sP.tag()] + (self.rw("endB_P") if ncn == 0 else []),
                     sem=self.sV)
            self.bset(bank, self.sV.tag())
        proj_done = self.sV.tag()

        for tt in range(NTT):
            par = tt % 2
            row0 = sc * MT + tt * 128
            xw = self.dmas("scalar", self.xt[par][:], d["x"][row0:row0 + 128, :], sem=self.sXT,
                           waits=self.rw("xtg"))
            att = self.attn_sb[par]
            for qb in range(4):
                bank = 3 + qb % 2
                bwt = self.bw(bank)
                for j in range(4):
                    ncn = qb * 4 + j
                    self.tr(self.pf[bank][:, j * 128:(j + 1) * 128],
                            self.attn_fm[:, ncn * MT + tt * 128:ncn * MT + (tt + 1) * 128],
                            ident=self.identf[:],
                            waits=(bwt + [proj_done]) if j == 0 else (),
                            sem=self.sP if j == 3 else None)
                self.cpy("vector", att[:, qb * 512:(qb + 1) * 512], self.pf[bank][:],
                         waits=[self.sP.tag()] +
                               (self.rw(f"attn{par}") if qb == 0 else []),
                         sem=self.sV)
                self.bset(bank, self.sV.tag())
            r_ap, b_ap = self._ln_coeffs(att[:], 1e-5)
            self.tt("vector", self.trash[:], att[:], self.bc(r_ap[:, 0:1], C),
                    ALU.mult, sem=self.sV)
            self.tt("vector", self.x2t[:], self.trash[:], self.xt[par][:], ALU.add,
                    waits=[xw] + self.rw("x2t") + self.rw("x2t_store"), sem=self.sV)
            self.rset("xtg", self.sV.tag())
            self.tt("vector", self.x2t[:], self.x2t[:], self.bc(b_ap[:, 0:1], C),
                    ALU.add, sem=self.sV)
            self.rset(f"attn{par}", self.sV.tag())
            x2w = self.dmas("scalar", self.x2s[row0:row0 + 128, :], self.x2t[:], sem=self.sX2,
                            waits=[self.sV.tag()])
            self.rset("x2t_store", x2w)
            r2, b2 = self._ln_coeffs(self.x2t[:], 1e-5)
            par_s = tt % 2
            self.tt("vector", self.trash[:], self.x2t[:], self.bc(r2[:, 0:1], C),
                    ALU.mult, sem=self.sV)
            self.tt("vector", self.smb[par_s][:], self.trash[:], self.bc(b2[:, 0:1], C),
                    ALU.add, waits=self.rw(f"smb{par_s}"), sem=self.sV)
            self.rset("x2t", self.sV.tag())
            self._transpose_to_fm(self.smb[par_s][:], self.fm2, tt, self.sV.tag())
            self.rset(f"smb{par_s}", self.sV.tag())
            if sc == 0 and tt == 0:
                self.dump("attn_t0", att[:], (128, C))
                self.dump("x2_t0", self.x2t[:], (128, C))
        self.ln2_ready = self.sV.tag()
        self.rset("endC_P", self.sP.tag())
        if sc == 0:
            self.dump("ln2_fm", self.fm2[:], (128, KC * MT), BF16)

    def emit_phaseD(self, sc):
        hsb_r = self.h_sbuf[:].rearrange("p (c t) -> p c t", t=MT)
        for s4 in range(4):
            for cgl in range(4):
                cg = s4 * 4 + cgl
                for r in range(4):
                    par = r % 2
                    gs, us = self.wa[par * 2], self.wa[par * 2 + 1]
                    cdone = self.cast_done_tags if (sc == 0 and cg == 0 and r == 0) else []
                    gw = self.dmas("sync", gs[:], self.cgw[cg * 4 + r], sem=self.sWA[2 * par],
                                   waits=self.slw((f"WA{2*par}",)) + cdone)
                    uw = self.dmas("scalar", us[:], self.cuw[cg * 4 + r],
                                   sem=self.sWA[2 * par + 1],
                                   waits=self.slw((f"WA{2*par+1}",)) + cdone)
                    for kc in range(KC):
                        self.mm(self.pf[par][:], gs[:, kc * 128:(kc + 1) * 128],
                                self.fm2[:, kc * MT:kc * MT + MT], kc == 0, kc == KC - 1,
                                waits=([gw, self.ln2_ready] + self.bw(par))
                                      if kc == 0 else (),
                                sem=self.sP if kc == KC - 1 else None)
                    gt = self.sP.tag()
                    for kc in range(KC):
                        self.mm(self.pf[2 + par][:], us[:, kc * 128:(kc + 1) * 128],
                                self.fm2[:, kc * MT:kc * MT + MT], kc == 0, kc == KC - 1,
                                waits=([uw] + self.bw(2 + par)) if kc == 0 else (),
                                sem=self.sP if kc == KC - 1 else None)
                    ut = self.sP.tag()
                    self.slset((f"WA{2*par}", f"WA{2*par+1}"), ut)
                    self.act(self.sg[par][:], self.pf[par][:], AF.Silu,
                             waits=[gt] + self.rw(f"sg{par}"), sem=self.sA)
                    self.bset(par, self.sA.tag())
                    self.tt("vector", self.O_r[:, r * MT:(r + 1) * MT], self.pf[2 + par][:],
                            self.sg[par][:], ALU.mult,
                            waits=[ut, self.sA.tag()] +
                                  ((self.rw("Or") + self.rw("x2t_store")) if r == 0 else []),
                            sem=self.sV)
                    self.rset(f"sg{par}", self.sV.tag())
                    self.bset(2 + par, self.sV.tag())
                vg = self.sV.tag()
                for r in range(4):
                    self.act(self.a_r[:, r * MT:(r + 1) * MT], self.O_r[:, r * MT:(r + 1) * MT],
                             AF.Abs, waits=([vg] + self.rw("ar")) if r == 0 else (), sem=self.sA)
                at = self.sA.tag()
                A = lambda r: self.a_r[:, r * MT:(r + 1) * MT]
                SP = lambda j: self.sp_r[:, j * MT:(j + 1) * MT]
                self.tt("vector", SP(0), A(0), A(1), ALU.max, waits=[at], sem=self.sV)
                self.tt("vector", SP(1), A(0), A(1), ALU.min, sem=self.sV)
                self.tt("vector", SP(2), A(2), A(3), ALU.max, sem=self.sV)
                self.tt("vector", SP(3), A(2), A(3), ALU.min, sem=self.sV)
                self.tt("vector", SP(0), SP(0), SP(2), ALU.min, sem=self.sV)
                self.tt("vector", SP(1), SP(1), SP(3), ALU.max, sem=self.sV)
                self.tt("vector", SP(0), SP(0), SP(1), ALU.max, sem=self.sV)
                for r in range(4):
                    self.tt("vector", SP(2), A(r), SP(0), ALU.is_ge, sem=self.sV)
                    self.tt("vector", self.h_sbuf[:, (cgl * 4 + r) * MT:(cgl * 4 + r + 1) * MT],
                            self.O_r[:, r * MT:(r + 1) * MT], SP(2), ALU.mult,
                            waits=(self.rw("hsb") + self.rw("endC_P"))
                                  if (cgl == 0 and r == 0) else (), sem=self.sV)
                self.rset("Or", self.sV.tag())
                self.rset("ar", self.sV.tag())
            hs_ready = self.sV.tag()
            if sc == 0 and s4 == 0:
                self.dump("hsb_sc0", self.h_sbuf[:], (128, 16 * MT), BF16)
                self.dump("Or_sc0", self.O_r[:], (128, 4 * MT))
                self.dump("ar_sc0", self.a_r[:], (128, 4 * MT))

            for n2h in range(2):
                for qtr in range(4):
                    par = qtr % 2
                    slab = self.wbq[2 * n2h + par]
                    dwt = self.dmas(("sync", "scalar")[(2 * n2h + par) % 2], slab[:],
                                    self.cdw[s4 * 8 + n2h * 4 + qtr],
                                    sem=self.sWB[2 * n2h + par],
                                    waits=self.slw((f"WB{2 * n2h + par}",)) +
                                          (self.cast_done_tags if (sc == 0 and s4 == 0 and
                                                                   n2h == 0 and qtr == 0) else []))
                    first_w = [dwt, hs_ready]
                    for tt in range(NTT):
                        for ic in range(4):
                            gic = qtr * 4 + ic
                            lhs = hsb_r[:, gic, tt * 128:(tt + 1) * 128]
                            for n2 in range(2):
                                bank = tt * 2 + n2
                                self.mm(self.pf[bank][:] if bank < 6 else self.pfd[bank - 6][:],
                                        lhs,
                                        slab[:, ic * 1024 + n2 * 512:ic * 1024 + (n2 + 1) * 512],
                                        start=(qtr == 0 and ic == 0), stop=(qtr == 3 and ic == 3),
                                        waits=(first_w + (self.bw(bank) if (qtr == 0 and ic == 0) else []))
                                              if (ic == 0) else (),
                                        sem=self.sP if (tt == 3 and ic == 3 and n2 == 1) else None)
                    self.slset((f"WB{2 * n2h + par}",), self.sP.tag())
                mt = self.sP.tag()
                for tt in range(NTT):
                    for n2 in range(2):
                        bank = tt * 2 + n2
                        src = self.pf[bank][:] if bank < 6 else self.pfd[bank - 6][:]
                        dst = self.acc[:, tt * C + (n2h * 2 + n2) * 512:
                                       tt * C + (n2h * 2 + n2 + 1) * 512]
                        if s4 == 0:
                            self.cpy("vector", dst, src,
                                     waits=([mt] + self.rw("accw")) if (tt == 0 and n2 == 0) else (),
                                     sem=self.sV)
                        else:
                            self.tt("vector", dst, dst, src, ALU.add,
                                    waits=[mt] if (tt == 0 and n2 == 0) else (), sem=self.sV)
                        self.bset(bank, self.sV.tag())
            self.rset("hsb", self.sP.tag())
            if sc == 0 and s4 == 0:
                self.dump("acc0_sc0", self.acc[:], (128, NTT * C))
        if sc == 0:
            self.dump("acc_sc0", self.acc[:], (128, NTT * C))
        for tt in range(NTT):
            par = tt % 2
            row0 = sc * MT + tt * 128
            xw = self.dmas("scalar", self.xt[par][:], self.x2s[row0:row0 + 128, :], sem=self.sXT,
                           waits=self.rw("xtg"))
            atile = self.acc[:, tt * C:(tt + 1) * C]
            r_ap, b_ap = self._ln_coeffs(atile, 1e-5)
            self.tt("vector", self.trash[:], atile, self.bc(r_ap[:, 0:1], C),
                    ALU.mult, sem=self.sV)
            self.tt("vector", atile, self.trash[:], self.xt[par][:], ALU.add,
                    waits=[xw], sem=self.sV)
            self.rset("xtg", self.sV.tag())
            self.tt("vector", atile, atile, self.bc(b_ap[:, 0:1], C), ALU.add, sem=self.sV)
            self.dmas("scalar", self.out[row0:row0 + 128, :], atile, sem=self.sOUT,
                      waits=[self.sV.tag()])
            if sc == 0 and tt == 0:
                self.dump("out_t0", atile, (128, C))
        self.rset("accw", self.sOUT.tag())
        self.rset("endD_P", self.sP.tag())
        self.rset("d_act_done", self.sA.tag())

    def build(self):
        self.emit_setup()
        if self.scope in ("full", "quant", "qc"):
            self.emit_quant()
        if self.scope == "qc":
            self._emit_casts(10 ** 9)
            if "cgw" in self.dbg:
                t = self.nc.dram_tensor("dbg_cgw", [64, 128, C], BF16,
                                        kind="ExternalOutput").ap()
                self.dbg_tensors["cgw"] = t
                for i in range(64):
                    self.dma("scalar", t[i], self.cgw[i], sem=self.sDBG,
                             waits=self.cast_done_tags)
            if "cdw2" in self.dbg:
                t = self.nc.dram_tensor("dbg_cdw2", [32, 128, 4096], BF16,
                                        kind="ExternalOutput").ap()
                self.dbg_tensors["cdw2"] = t
                for i in range(32):
                    self.dma("scalar", t[i], self.cdw[i], sem=self.sDBG,
                             waits=self.cast_done_tags)
        if self.scope == "full":
            npsc = max(1, self.n_prol)
            per = (len(self.cast_tasks) + npsc - 1) // npsc
            for psc in range(self.n_prol):
                self.emit_phaseA(psc, prologue=True)
                self._emit_casts(per // 2)
                self.emit_phaseB(psc, prologue=True, gchunk0=psc * NCH)
                self._emit_casts(per - per // 2)
            self._emit_casts(10 ** 9)
            g0 = self.n_prol * NCH
            for sc in range(self.n_main):
                self.emit_phaseA(sc, prologue=False)
                self.emit_phaseB(sc, prologue=False, gchunk0=g0 + sc * NCH)
                self.emit_phaseC(sc)
                self.emit_phaseD(sc)
        fin = [(s.h, s.n) for s in self.all_sems if s.n > 0]
        clr = [s.h for s in self.all_sems]
        def f(e, fin=tuple(fin), clr=tuple(clr)):
            for hh, v in fin:
                e.wait_ge(hh, v)
            for hh in clr:
                e.sem_clear(hh)
        self._app("sync", f)

        nc = self.nc
        with nc.Block() as blk:
            @blk.sync
            def _(e):
                for fn in self.st["sync"]:
                    fn(e)

            @blk.tensor
            def _(e):
                for fn in self.st["tensor"]:
                    fn(e)

            @blk.vector
            def _(e):
                for fn in self.st["vector"]:
                    fn(e)

            @blk.scalar
            def _(e):
                for fn in self.st["scalar"]:
                    fn(e)

            @blk.gpsimd
            def _(e):
                for fn in self.st["gpsimd"]:
                    fn(e)
        return nc


def kernel(**inputs):
    in_maps = _host_prep(inputs)
    nc = KB().build()
    res = run_bass_kernel_spmd(nc, in_maps, core_ids=list(range(N_CORES)))
    out = np.empty((B_, SEQ, C), np.float32)
    for core in range(N_CORES):
        b, hf = core // 2, core % 2
        out[b, hf * M:(hf + 1) * M] = res.results[core]["out"]
    return out


# version 11
# speedup vs baseline: 1.0905x; 1.0905x over previous
import numpy as np
import ml_dtypes
import concourse.bass as bass
import concourse.mybir as mybir
from concourse.bass_utils import run_bass_kernel_spmd

F32 = mybir.dt.float32
BF16 = mybir.dt.bfloat16
AF = mybir.ActivationFunctionType
ALU = mybir.AluOpType
AX = mybir.AxisListType

B_, SEQ, C = 4, 8192, 2048
H, D, T = 16, 128, 128
M = 4096
MT = 512
G = M // MT
NTT = MT // 128
NCH = MT // T
KC = C // 128
I = 8192
EPS32 = float(np.finfo(np.float32).eps)
SCALE = float(D) ** -0.5
N_CORES = 8
BF = ml_dtypes.bfloat16


def _host_prep(inputs):
    x = np.ascontiguousarray(np.asarray(inputs["L0_attn_x"], np.float32))
    qkv_w = np.asarray(inputs["L0_attn_qkv_w"], np.float32)
    qkv_norm_w = np.asarray(inputs["L0_attn_qkv_norm_w"], np.float32)
    proj_w = np.asarray(inputs["L0_attn_proj_w"], np.float32)
    proj_norm_w = np.asarray(inputs["L0_attn_proj_norm_w"], np.float32)
    norm_w = np.asarray(inputs["L0_attn_norm_w"], np.float32)
    s = np.asarray(inputs["L0_attn_s"], np.float32)
    w_gate = np.asarray(inputs["L0_ffn_w_gate"], np.float32)
    w_up = np.asarray(inputs["L0_ffn_w_up"], np.float32)
    w_down = np.asarray(inputs["L0_ffn_w_down"], np.float32)

    qkvT = np.ascontiguousarray(
        qkv_w[:4096].reshape(32, 128, KC, 128).transpose(0, 3, 2, 1).reshape(32, 128, C))
    vT = np.ascontiguousarray(qkv_w[4096:].T.reshape(KC, 128, C))
    pT = np.ascontiguousarray(
        proj_w.reshape(16, 128, KC, 128).transpose(0, 3, 2, 1).reshape(16, 128, C))
    gq = np.arange(16)[:, None, None]
    r4 = np.arange(4)[None, :, None]
    n128 = np.arange(128)[None, None, :]
    perm_cols = (gq * 512 + n128 * 4 + r4).reshape(-1)
    gT = np.ascontiguousarray(
        w_gate[:, perm_cols].reshape(KC, 128, 64, 128).transpose(2, 1, 0, 3).reshape(64, 128, C))
    uT = np.ascontiguousarray(
        w_up[:, perm_cols].reshape(KC, 128, 64, 128).transpose(2, 1, 0, 3).reshape(64, 128, C))
    c2 = np.arange(64)[:, None]
    p128 = np.arange(128)[None, :]
    rows_perm = ((c2 // 4) * 512 + p128 * 4 + (c2 % 4)).reshape(-1)
    wdp = w_down[rows_perm, :]
    d6 = wdp.reshape(4, 4, 4, 128, 2, 2, 512)
    dT = np.ascontiguousarray(
        d6.transpose(0, 4, 1, 3, 2, 5, 6).reshape(32, 128, 4096))

    qnw = np.ascontiguousarray(qkv_norm_w.reshape(KC, 128).T)
    nw16 = np.ascontiguousarray(norm_w.reshape(KC, 128).T)
    pnw16 = np.ascontiguousarray(proj_norm_w.reshape(KC, 128).T)
    nwrow = np.ascontiguousarray(norm_w[None, :])
    srow = np.ascontiguousarray(s[None, :])

    ar = np.arange(128, dtype=np.float32)
    consts = {
        "identb": np.eye(128, dtype=np.float32).astype(BF),
        "identf": np.eye(128, dtype=np.float32),
        "diffm": np.maximum(ar[None, :] - ar[:, None], 0.0).astype(np.float32),
        "tril": (ar[None, :] >= ar[:, None]).astype(np.float32),
        "rampc": ar.reshape(128, 1).copy(),
        "ramprc": (T - ar).reshape(128, 1).astype(np.float32),
        "rampc16": np.tile(ar.reshape(128, 1), (1, 16)).astype(np.float32),
        "ramprc16": np.tile((T - ar).reshape(128, 1), (1, 16)).astype(np.float32),
        "ones1rf": np.ones((1, 128), np.float32),
        "onescf": np.ones((128, 1), np.float32),
        "fconsts": np.tile(np.array([[1e-5, 1e-6, EPS32, -0.5, -1.0, -128.0, SCALE,
                                      1e-5 * EPS32]], np.float32), (128, 1)),
    }
    shared = dict(qkvT=qkvT, vT=vT, pT=pT, gT=gT, uT=uT, dT=dT, qnw=qnw, nw16=nw16,
                  pnw16=pnw16, nwrow=nwrow, srow=srow, **consts)
    in_maps = []
    for core in range(N_CORES):
        b, hf = core // 2, core % 2
        xc = np.ascontiguousarray(x[b, hf * M:(hf + 1) * M])
        xp = (np.ascontiguousarray(x[b, 0:M]) if hf == 1 else np.zeros((M, C), np.float32))
        in_maps.append(dict(x=xc, xprev=xp, **shared))
    return in_maps


class _Sem:
    def __init__(self, h):
        self.h = h
        self.n = 0

    def tag(self):
        return (self, self.n)


class KB:
    def __init__(self, n_prol=G, n_main=G, scope="full", dbg=(), scratch_kind=None):
        self.dbg = set(dbg)
        self.n_prol = n_prol
        self.n_main = n_main
        self.scope = scope
        nc = self.nc = bass.Bass("TRN2", target_bir_lowering=False, debug=False)
        self.st = {k: [] for k in ("sync", "tensor", "vector", "scalar", "gpsimd")}

        dram = {}
        def din(name, shape, dt=F32):
            dram[name] = nc.dram_tensor(name, list(shape), dt, kind="ExternalInput").ap()
        din("x", (M, C)); din("xprev", (M, C))
        din("qkvT", (32, 128, C)); din("vT", (KC, 128, C)); din("pT", (KC, 128, C))
        din("gT", (64, 128, C)); din("uT", (64, 128, C)); din("dT", (32, 128, 4096))
        din("qnw", (128, 16)); din("nw16", (128, 16)); din("pnw16", (128, 16))
        din("nwrow", (1, C)); din("srow", (1, 16))
        din("identb", (128, 128), BF16); din("identf", (128, 128))
        din("diffm", (128, 128)); din("tril", (128, 128))
        din("rampc", (128, 1)); din("ramprc", (128, 1))
        din("rampc16", (128, 16)); din("ramprc16", (128, 16))
        din("ones1rf", (1, 128)); din("onescf", (128, 1))
        din("fconsts", (128, 8))
        self.dr = dram
        self.out = nc.dram_tensor("out", [M, C], F32, kind="ExternalOutput").ap()
        ckind = "Internal" if scratch_kind is None else scratch_kind
        self.cqkv = nc.dram_tensor("cqkv", [32, 128, C], BF16, kind=ckind).ap()
        self.cv = nc.dram_tensor("cv", [KC, 128, C], BF16, kind=ckind).ap()
        self.cpw = nc.dram_tensor("cpw", [KC, 128, C], BF16, kind=ckind).ap()
        self.cgw = nc.dram_tensor("cgw", [64, 128, C], BF16, kind=ckind).ap()
        self.cuw = nc.dram_tensor("cuw", [64, 128, C], BF16, kind=ckind).ap()
        self.cdw = nc.dram_tensor("cdw", [32, 128, 4096], BF16, kind=ckind).ap()
        self.x2s = nc.dram_tensor("x2s", [M, C], F32, kind=ckind).ap()
        self.bnc1 = nc.dram_tensor("bnc1", [16, 128], F32, kind=ckind).ap()
        self.bnc2 = nc.dram_tensor("bnc2", [16, 128], F32, kind=ckind).ap()

        self.cur = (int(nc.sbuf_base) + 31) // 32 * 32
        def sb(name, shape, dt=F32):
            nbytes = int(np.prod(shape[1:])) * (2 if dt == BF16 else 4)
            t = nc.alloc_sbuf_tensor_at(name, list(shape), dt, offset=self.cur)
            self.cur += (nbytes + 31) // 32 * 32
            return t
        def sb_at(name, shape, dt, off):
            return nc.alloc_sbuf_tensor_at(name, list(shape), dt, offset=off)

        self.identb = sb("s_identb", (128, 128), BF16)
        self.identf = sb("s_identf", (128, 128))
        self.diffm = sb("s_diffm", (128, 128))
        self.tril = sb("s_tril", (128, 128))
        self.masksT = sb("s_masksT", (128, H * 128), BF16)
        self.lam_t_bc = sb("s_lam_t_bc", (128, H * 128), BF16)
        self.nw2_bc = sb("s_nw2_bc", (128, C))
        self.S = sb("s_S", (128, H * 128))
        self.Sbf = [sb("s_Sbf0", (128, H * 128), BF16),
                    sb("s_Sbf1", (128, H * 128), BF16)]
        for nm in ("s_bc", "nsbc", "lam_t16", "lam_rev16", "lamT16",
                   "qnw_s", "nw_s", "pnw_s", "rfq", "rfq2", "rfp", "rampc16_s", "ramprc16_s"):
            setattr(self, nm, sb("s_" + nm, (128, 16)))
        for nm in ("rampc_s", "ramprc_s", "onescf_s", "accq", "accp", "accqB", "accpB",
                   "redt", "redtB", "mq", "mp", "thrq", "nthrq", "thrp", "nthrp"):
            setattr(self, nm, sb("s_" + nm, (128, 1)))
        self.fconsts = sb("s_fconsts", (128, 8))
        for _j, _v in enumerate((1e-5, 1e-6, EPS32, -0.5, -1.0, -128.0, SCALE,
                                 1e-5 * EPS32)):
            nc.const_aps.register(F32, float(_v), self.fconsts[:, _j:_j + 1])
        self.ones1rf_s = sb("s_ones1rf", (1, 128))
        self.srow_s = sb("s_srow", (1, 16))
        self.stot = sb("s_stot", (1, 1))
        self.stn = [sb(f"s_st{i}", (128, 1)) for i in range(12)]

        self.off_U1 = self.cur; self.cur += 73728
        self.off_FM2 = self.cur; self.cur += 16384
        self.off_XT = self.cur; self.cur += 8192
        self.off_TRASH = self.cur; self.cur += 8192
        self.off_WA = self.cur; self.cur += 16384
        self.off_WB = self.cur; self.cur += 32768
        self.off_SMB = self.cur; self.cur += 8192
        self.off_ATTS = self.cur; self.cur += 8192
        assert self.cur <= int(nc.sbuf_top), f"SBUF over budget: {self.cur} > {nc.sbuf_top}"
        U1, FM2, XT, TR = self.off_U1, self.off_FM2, self.off_XT, self.off_TRASH
        WA, WB, SMB, AT = self.off_WA, self.off_WB, self.off_SMB, self.off_ATTS

        self.h_fm = sb_at("v_h_fm", (128, KC * MT), BF16, U1)
        self.q_fm = sb_at("v_q_fm", (128, H * MT), BF16, U1 + 16384)
        self.k_fm = sb_at("v_k_fm", (128, H * MT), BF16, U1 + 32768)
        self.v_tm = sb_at("v_v_tm", (128, H * MT), BF16, U1 + 49152)
        self.o_sb = [sb_at("v_o0", (128, C), F32, U1),
                     sb_at("v_o1", (128, C), F32, U1 + 8192)]
        self.attn_fm = sb_at("v_atfm", (128, 16 * MT), F32, U1)
        self.attn_sb = [sb_at("v_at0", (128, C), F32, U1 + 32768),
                        sb_at("v_at1", (128, C), F32, U1 + 40960)]
        self.x2t = sb_at("v_x2t", (128, C), F32, U1 + 49152)
        self.h_sbuf = sb_at("v_hsb", (128, 16 * MT), BF16, U1)
        self.acc = sb_at("v_acc", (128, NTT * C), F32, U1 + 16384)
        self.O_r = sb_at("v_Or", (128, 4 * MT), F32, U1 + 49152)
        self.a_r = sb_at("v_ar", (128, 4 * MT), F32, U1 + 57344)
        self.sp_r = sb_at("v_spr", (128, 4 * MT), F32, U1 + 65536)
        self.sg = [sb_at("v_sg0", (128, MT), F32, TR),
                   sb_at("v_sg1", (128, MT), F32, TR + 2048)]
        self.qgp = [sb_at(f"v_qgp{i}", (128, C), F32, U1 + i * 8192) for i in range(4)]
        self.qgn = [sb_at(f"v_qgn{i}", (128, C), F32, U1 + 32768 + i * 8192) for i in range(4)]
        self.redcols = sb_at("v_redcols", (128, 64), F32, U1 + 49152)
        self.cfp = [sb_at("v_cfp0", (128, C), F32, U1 + 16384),
                    sb_at("v_cfp1", (128, C), F32, U1 + 24576),
                    sb_at("v_cfp2", (128, C), F32, FM2)]
        self.cbf = [sb_at("v_cbf0", (128, C), BF16, U1 + 65536),
                    sb_at("v_cbf1", (128, C), BF16, U1 + 69632),
                    sb_at("v_cbf2", (128, C), BF16, FM2 + 8192),
                    sb_at("v_cbf3", (128, C), BF16, FM2 + 12288)]
        self.fm2 = sb_at("v_fm2", (128, KC * MT), BF16, FM2)
        _xt = sb_at("v_xt0", (128, C), F32, XT)
        self.xt = [_xt, _xt]
        self.trash = sb_at("v_trash", (128, C), F32, TR)
        self.lamrow = sb_at("v_lamrow", (1, H * 128), F32, TR)
        self.nwrow_s = sb_at("v_nwrow", (1, C), F32, TR)
        self.wa = [sb_at(f"v_wa{i}", (128, C), BF16, WA + i * 4096) for i in range(4)]
        self.waf = [sb_at("v_waf0", (128, C), F32, WA), sb_at("v_waf1", (128, C), F32, WA + 8192)]
        self.wbf = [sb_at("v_wbf0", (128, C), F32, WB), sb_at("v_wbf1", (128, C), F32, WB + 8192)]
        self.effx = [sb_at("v_effx0", (128, C), BF16, AT), sb_at("v_effx1", (128, C), BF16, AT + 4096)]
        self.wb = sb_at("v_wb", (128, 2 * I), BF16, WB)
        self.wbq = [sb_at(f"v_wbq{i}", (128, 4096), BF16, WB + i * 8192) for i in range(4)]
        self.smb = [sb_at("v_smb0", (128, C), BF16, SMB), sb_at("v_smb1", (128, C), BF16, SMB + 4096)]
        self.atts = {}
        for j, nm in enumerate(("sT", "ktr", "vtr", "qlam")):
            self.atts[nm] = [sb_at(f"v_{nm}0", (128, 512), BF16, AT + j * 2048),
                             sb_at(f"v_{nm}1", (128, 512), BF16, AT + j * 2048 + 1024)]

        self.pf = [nc.alloc_psum_tensor(f"pf{i}", [128, 512], F32) for i in range(6)]
        self.pt = [nc.alloc_psum_tensor(f"pt{i}", [128, 1024], BF16) for i in range(2)]
        nc.psum_base = 6
        self.pfd = [nc.alloc_psum_tensor(f"pfd{i}", [128, 512], F32) for i in range(2)]
        self.ptf = self.pfd[0]

        self.sCONST = _Sem(nc.alloc_semaphore("sCONST"))
        self.sBNC = _Sem(nc.alloc_semaphore("sBNC"))
        self.sWA = [_Sem(nc.alloc_semaphore(f"sWA{i}")) for i in range(4)]
        self.sWB = [_Sem(nc.alloc_semaphore(f"sWB{i}")) for i in range(4)]
        self.sXT = _Sem(nc.alloc_semaphore("sXT"))
        self.sSMB = [_Sem(nc.alloc_semaphore(f"sSMB{i}")) for i in range(2)]
        self.sDRAW = _Sem(nc.alloc_semaphore("sDRAW"))
        self.sDBF = _Sem(nc.alloc_semaphore("sDBF"))
        self.sCLD = [_Sem(nc.alloc_semaphore(f"sCLD{i}")) for i in range(4)]
        self.sCST = [_Sem(nc.alloc_semaphore(f"sCST{i}")) for i in range(4)]
        self.sX2 = _Sem(nc.alloc_semaphore("sX2"))
        self.sOUT = _Sem(nc.alloc_semaphore("sOUT"))
        self.sDBG = _Sem(nc.alloc_semaphore("sDBG"))
        self.sP = _Sem(nc.alloc_semaphore("sP"))
        self.sV = _Sem(nc.alloc_semaphore("sV"))
        self.sA = _Sem(nc.alloc_semaphore("sA"))
        self.sO = _Sem(nc.alloc_semaphore("sO"))
        self.all_dma_sems = ([self.sCONST, self.sBNC, self.sXT, self.sDRAW, self.sDBF,
                              self.sX2, self.sOUT, self.sDBG] + self.sWA + self.sWB +
                             self.sSMB + self.sCLD + self.sCST)
        self.all_sems = self.all_dma_sems + [self.sP, self.sV, self.sA, self.sO]

        self.bguard = {}
        self.rguard = {}
        self.dbg_tensors = {}

    def _app(self, eng, f):
        self.st[eng].append(f)

    def ins(self, eng, builder, waits=(), sem=None, inc=1):
        waits = tuple((s.h, v) for (s, v) in waits if v > 0)
        if sem is not None:
            sem.n += inc
        def f(e, b=builder, waits=waits, sh=(sem.h if sem else None), inc=inc):
            for hh, v in waits:
                e.wait_ge(hh, v)
            i = b(e)
            if sh is not None:
                i.then_inc(sh, inc)
        self._app(eng, f)
        return sem.n if sem else None

    @staticmethod
    def bc(col, n):
        return col.broadcast_to([128, n])

    def pad(self, eng, n=10):
        for _ in range(n):
            self.ins(eng, (lambda e: e.engine_nop()))

    def vfence(self):
        t = self.sV.tag()
        self.ins("vector", (lambda e: e.engine_nop()), waits=[t], sem=self.sV)

    def dma(self, q, out, in_, sem=None, waits=()):
        self.ins(q, (lambda e, o=out, ii=in_: e.dma_start(out=o, in_=ii)),
                 waits=waits, sem=sem, inc=16)
        return (sem, sem.n)

    def dmas(self, q, out, in_, sem=None, waits=(), nsplit=1):
        if nsplit == 1:
            return self.dma(q, out, in_, sem=sem, waits=waits)
        ncol = out.shape[-1]
        w = ncol // nsplit
        for k in range(nsplit):
            self.ins(q, (lambda e, o=out[:, k * w:(k + 1) * w],
                         ii=in_[:, k * w:(k + 1) * w]: e.dma_start(out=o, in_=ii)),
                     waits=(waits if k == 0 else ()), sem=sem, inc=16)
        return (sem, sem.n)

    def barrier(self, engines=("tensor", "vector", "scalar", "gpsimd")):
        tags = [self.sP.tag(), self.sV.tag(), self.sA.tag(), self.sO.tag()]
        tags = [(s.h, v) for (s, v) in tags if v > 0]
        for eng in engines:
            def f(e, tags=tuple(tags)):
                for hh, v in tags:
                    e.wait_ge(hh, v)
            self._app(eng, f)

    def mm(self, out, lhsT, rhs, start, stop, waits=(), sem=None, inc=1):
        return self.ins("tensor", (lambda e, o=out, l=lhsT, r=rhs, st=start, sp=stop:
                                   e.matmul(o, l, r, start=st, stop=sp)),
                        waits=waits, sem=sem, inc=inc)

    def tr(self, out, in_, ident=None, waits=(), sem=None, inc=1):
        idt = (ident if ident is not None else self.identb[:])
        return self.ins("tensor", (lambda e, o=out, ii=in_, idt=idt: e.transpose(o, ii, idt)),
                        waits=waits, sem=sem, inc=inc)

    def cpy(self, eng, out, in_, waits=(), sem=None, inc=1):
        return self.ins(eng, (lambda e, o=out, ii=in_: e.tensor_copy(o, ii)),
                        waits=waits, sem=sem, inc=inc)

    def tt(self, eng, out, a, b, op, waits=(), sem=None, inc=1):
        return self.ins(eng, (lambda e, o=out, x=a, y=b, op=op: e.tensor_tensor(o, x, y, op)),
                        waits=waits, sem=sem, inc=inc)

    def ts(self, eng, out, a, s1, op0, s2=None, op1=None, waits=(), sem=None, inc=1):
        if s2 is None:
            return self.ins(eng, (lambda e, o=out, x=a, s1=s1, op0=op0:
                                  e.tensor_single_scalar(o, x, s1, op0)),
                            waits=waits, sem=sem, inc=inc)
        return self.ins(eng, (lambda e, o=out, x=a, s1=s1, s2=s2, op0=op0, op1=op1:
                              e.tensor_scalar(o, x, s1, s2, op0, op1)),
                        waits=waits, sem=sem, inc=inc)

    def stt(self, eng, out, in0, scalar, in1, op0, op1, accum=None, waits=(), sem=None, inc=1):
        return self.ins(eng, (lambda e, o=out, a=in0, s=scalar, b=in1, op0=op0, op1=op1, ac=accum:
                              e.scalar_tensor_tensor(o, a, s, b, op0, op1, accum_out=ac)),
                        waits=waits, sem=sem, inc=inc)

    def act(self, out, in_, func, bias=0.0, scale=1.0, accum=None, waits=(), sem=None, inc=1):
        return self.ins("scalar", (lambda e, o=out, ii=in_, fn=func, b=bias, s=scale, ac=accum:
                                   e.activation(o, ii, fn, bias=b, scale=s, accum_out=ac)),
                        waits=waits, sem=sem, inc=inc)

    def acopy(self, out, in_, waits=(), sem=None, inc=1):
        return self.ins("scalar", (lambda e, o=out, ii=in_: e.copy(o, ii)),
                        waits=waits, sem=sem, inc=inc)

    def red(self, eng, out, in_, op=ALU.add, absval=False, waits=(), sem=None, inc=1):
        return self.ins(eng, (lambda e, o=out, ii=in_, op=op, av=absval:
                              e.tensor_reduce(o, ii, AX.X, op, apply_absolute_value=av)),
                        waits=waits, sem=sem, inc=inc)

    def memset(self, eng, ap, v, waits=(), sem=None, inc=1):
        return self.ins(eng, (lambda e, a=ap, v=v: e.memset(a, v)),
                        waits=waits, sem=sem, inc=inc)

    def dump(self, name, ap, shape, dt=F32):
        if name not in self.dbg:
            return
        t = self.nc.dram_tensor("dbg_" + name, list(shape), dt, kind="ExternalOutput").ap()
        self.dbg_tensors[name] = t
        waits = [t2 for t2 in (self.sV.tag(), self.sA.tag(), self.sP.tag(), self.sO.tag())]
        wt = self.dma("scalar", t, ap, sem=self.sDBG, waits=waits)
        for eng in ("tensor", "vector", "scalar", "gpsimd", "sync"):
            self.ins(eng, (lambda e: e.nop()), waits=[wt])

    def bw(self, *banks):
        out = []
        for b in banks:
            g = self.bguard.get(b)
            if g:
                out.append(g)
        return out

    def bset(self, bank, tag):
        self.bguard[bank] = tag

    def rw(self, name):
        g = self.rguard.get(name)
        return [g] if g else []

    def rset(self, name, tag):
        self.rguard[name] = tag

    def slw(self, slots):
        out = []
        for s in slots:
            g = self.rguard.get(s)
            if g:
                out.append(g)
        return out

    def slset(self, slots, tag):
        for s in slots:
            self.rguard[s] = tag

    def emit_setup(self):
        d = self.dr
        last = None
        for nm, dst in (("identb", self.identb), ("identf", self.identf),
                        ("diffm", self.diffm), ("tril", self.tril),
                        ("qnw", self.qnw_s), ("nw16", self.nw_s), ("pnw16", self.pnw_s),
                        ("rampc", self.rampc_s), ("ramprc", self.ramprc_s),
                        ("rampc16", self.rampc16_s), ("ramprc16", self.ramprc16_s),
                        ("onescf", self.onescf_s), ("ones1rf", self.ones1rf_s),
                        ("srow", self.srow_s), ("fconsts", self.fconsts)):
            last = self.dma("sync", dst[:], d[nm], sem=self.sCONST)
        wld = [last]

        self.mm(self.ptf[:, 0:16], self.ones1rf_s[:], self.srow_s[:], True, True,
                waits=wld, sem=self.sP)
        self.cpy("vector", self.s_bc[:], self.ptf[:, 0:16], waits=[self.sP.tag()], sem=self.sV)
        self.bset(6, self.sV.tag())
        self.vfence()
        self.ts("vector", self.nsbc[:], self.s_bc[:], -1.0, ALU.mult, sem=self.sV)
        self.ts("vector", self.rfq[:], self.s_bc[:], -float(T), ALU.mult, sem=self.sV)
        self.vfence()
        self.tt("vector", self.lam_t16[:], self.nsbc[:], self.rampc16_s[:], ALU.mult, sem=self.sV)
        self.tt("vector", self.lam_rev16[:], self.nsbc[:], self.ramprc16_s[:], ALU.mult, sem=self.sV)
        self.vfence()
        va = self.sV.tag()
        self.act(self.lam_t16[:], self.lam_t16[:], AF.Exp, waits=[va], sem=self.sA)
        self.act(self.lam_rev16[:], self.lam_rev16[:], AF.Exp, sem=self.sA)
        self.act(self.lamT16[:], self.rfq[:], AF.Exp, sem=self.sA)
        aa = self.sA.tag()

        d3 = self.diffm[:][:, None, :].broadcast_to([128, 16, 128])
        n3 = self.nsbc[:][:, :, None].broadcast_to([128, 16, 128])
        t3 = self.trash[:].rearrange("p (h t) -> p h t", t=128)
        self.tt("vector", t3, d3, n3, ALU.mult, sem=self.sV)
        self.act(self.masksT[:], self.trash[:], AF.Exp, waits=[self.sV.tag()], sem=self.sA)
        tr3 = self.tril[:][:, None, :].broadcast_to([128, 16, 128])
        m3 = self.masksT[:].rearrange("p (h t) -> p h t", t=128)
        self.tt("vector", m3, m3, tr3, ALU.mult, waits=[self.sA.tag()], sem=self.sV)

        for (src, bncT, dst) in ((self.lam_t16, self.bnc1, self.lam_t_bc),):
            self.tr(self.ptf[0:16, 0:128], src[:, 0:16], ident=self.identf[:],
                    waits=[self.sA.tag()] + self.bw(6), sem=self.sP)
            self.cpy("vector", self.trash[0:16, 0:128], self.ptf[0:16, 0:128],
                     waits=[self.sP.tag()], sem=self.sV)
            self.bset(6, self.sV.tag())
            wb1 = self.dma("sync", bncT, self.trash[0:16, 0:128], sem=self.sBNC,
                           waits=[self.sV.tag()])
            wrow = self.dma("sync", self.lamrow[:],
                            bncT.rearrange("h t -> (h t)")[None, :],
                            sem=self.sBNC, waits=[wb1])
            for j in range(4):
                self.mm(self.pf[j][:], self.ones1rf_s[:], self.lamrow[:, j * 512:(j + 1) * 512],
                        True, True, waits=([wrow] + self.bw(j)), sem=self.sP)
            mt = self.sP.tag()
            for j in range(4):
                self.cpy("vector", dst[:, j * 512:(j + 1) * 512], self.pf[j][:],
                         waits=[mt] if j == 0 else (), sem=self.sV)
                self.bset(j, self.sV.tag())
            self.rset("lamrow", self.sV.tag())

        nwl = self.dma("sync", self.nwrow_s[:], d["nwrow"], sem=self.sCONST,
                       waits=self.rw("lamrow"))
        for j in range(4):
            self.mm(self.pf[j][:], self.ones1rf_s[:], self.nwrow_s[:, j * 512:(j + 1) * 512],
                    True, True, waits=([nwl] + self.bw(j)), sem=self.sP)
        mt = self.sP.tag()
        for j in range(4):
            self.cpy("vector", self.nw2_bc[:, j * 512:(j + 1) * 512], self.pf[j][:],
                     waits=[mt] if j == 0 else (), sem=self.sV)
            self.bset(j, self.sV.tag())
        self.tt("vector", self.nw2_bc[:], self.nw2_bc[:], self.nw2_bc[:], ALU.mult, sem=self.sV)

        self.memset("vector", self.S[:], 0.0, sem=self.sV)
        self.memset("vector", self.Sbf[0][:], 0.0, sem=self.sV)
        self.memset("vector", self.accq[:], 0.0, sem=self.sV)
        self.memset("vector", self.accp[:], 0.0, sem=self.sV)
        self.dump("accq0", self.accq[:], (128, 1))
        self.dump("smalls0", self.stn[0][:], (128, 1))
        self.dump("masksT", self.masksT[:], (128, H * 128), BF16)
        self.dump("lam_t_bc", self.lam_t_bc[:], (128, H * 128), BF16)

    def emit_quant(self):
        d = self.dr
        qslab = [self.waf[0], self.waf[1], self.wbf[0], self.wbf[1]]
        for grp, n, base in (("qkvT", 32, 0), ("vT", KC, 32), ("pT", KC, 48)):
            for i in range(n):
                j = (i + base) % 4
                wslab = qslab[j]
                wt = self.dmas(("sync", "scalar")[j % 2], wslab[:], d[grp][i],
                               sem=self.sWA[j], waits=self.slw((f"QW{j}",)))
                col = base + i
                self.red("vector", self.redcols[:, col:col + 1], wslab[:], absval=True,
                         waits=[wt], sem=self.sV)
                self.slset((f"QW{j}",), self.sV.tag())
        self.red("vector", self.accq[:], self.redcols[:, 0:48], sem=self.sV)
        self.red("vector", self.accp[:], self.redcols[:, 48:64], sem=self.sV)
        self.vfence()
        self.dump("redcols", self.redcols[:], (128, 64))
        self.dump("accq", self.accq[:], (128, 1))
        self.dump("accp", self.accp[:], (128, 1))
        for acc, mcol, nelem in ((self.accq, self.mq, 6144 * 2048),
                                 (self.accp, self.mp, 2048 * 2048)):
            self.mm(self.ptf[0:1, 0:1], acc[:], self.onescf_s[:], True, True,
                    waits=[self.sV.tag()] + self.bw(6), sem=self.sP)
            self.cpy("vector", self.stot[:], self.ptf[0:1, 0:1], waits=[self.sP.tag()], sem=self.sV)
            self.vfence()
            self.mm(self.ptf[:, 4:5], self.ones1rf_s[:], self.stot[:], True, True,
                    waits=[self.sV.tag()], sem=self.sP)
            self.ts("vector", mcol[:], self.ptf[:, 4:5], 1.0 / nelem, ALU.mult, 1e-5, ALU.max,
                    waits=[self.sP.tag()], sem=self.sV)
            self.vfence()
            self.bset(6, self.sV.tag())
        self.ts("vector", self.thrq[:], self.mq[:], 0.5, ALU.mult, sem=self.sV)
        self.ts("vector", self.nthrq[:], self.mq[:], -0.5, ALU.mult, sem=self.sV)
        self.ts("vector", self.thrp[:], self.mp[:], 0.5, ALU.mult, sem=self.sV)
        self.ts("vector", self.nthrp[:], self.mp[:], -0.5, ALU.mult, sem=self.sV)
        self.tt("vector", self.rfp[:], self.nw_s[:], self.pnw_s[:], ALU.mult, sem=self.sV)
        self.tt("vector", self.rfq[:], self.qnw_s[:], self.bc(self.mq[:, 0:1], 16),
                ALU.mult, sem=self.sV)
        self.vfence()
        self.ts("vector", self.rfq2[:], self.rfq[:], SCALE, ALU.mult, sem=self.sV)
        self.tt("vector", self.rfp[:], self.rfp[:], self.bc(self.mp[:, 0:1], 16),
                ALU.mult, sem=self.sV)
        self.vfence()
        thr_tag = self.sV.tag()
        self.dump("mq", self.mq[:], (128, 1))
        self.dump("mp", self.mp[:], (128, 1))

        effs = [self.smb[0], self.smb[1], self.effx[0], self.effx[1]]
        effg = ["smb0", "smb1", "eff2", "eff3"]
        effsem = [self.sSMB[0], self.sSMB[1], self.sCST[0], self.sCST[1]]
        last_eff = [None] * 4
        for grp, n, base, cache, thr, nthr, rf, kind in (
                ("qkvT", 32, 0, self.cqkv, self.thrq, self.nthrq, self.rfq, "qk"),
                ("vT", KC, 32, self.cv, self.thrq, self.nthrq, self.rfq, "v"),
                ("pT", KC, 48, self.cpw, self.thrp, self.nthrp, self.rfp, "p")):
            for i in range(n):
                ii = i + base
                j = ii % 4
                par = ii % 4
                wslab = qslab[j]
                wt = self.dmas(("sync", "scalar")[j % 2], wslab[:], d[grp][i],
                               sem=self.sWA[j], waits=self.slw((f"QW{j}", f"QWv{j}")))
                qgp, qgn = self.qgp[par], self.qgn[par]
                self.ts("vector", qgp[:], wslab[:], thr[:, 0:1], ALU.is_gt,
                        waits=[wt] + self.rw(f"qg{par}"), sem=self.sV)
                self.ts("vector", qgn[:], wslab[:], nthr[:, 0:1], ALU.is_lt, sem=self.sV)
                self.rguard[f"QW{j}"] = self.sV.tag()
                self.tt("vector", qgp[:], qgp[:], qgn[:], ALU.subtract, sem=self.sV)
                eff = effs[j]
                if kind == "v":
                    self.ts("vector", eff[:], qgp[:], self.rfq[:, i:i + 1], ALU.mult,
                            waits=self.rw(effg[j]), sem=self.sV)
                else:
                    g3 = qgp[:].rearrange("p (kc n) -> p kc n", n=128)
                    e3 = eff[:].rearrange("p (kc n) -> p kc n", n=128)
                    rfu = self.rfq2 if (kind == "qk" and i < 16) else rf
                    r3 = rfu[:][:, :, None].broadcast_to([128, KC, 128])
                    self.tt("vector", e3, g3, r3, ALU.mult,
                            waits=self.rw(effg[j]), sem=self.sV)
                self.rset(f"qg{par}", self.sV.tag())
                last_eff[j] = self.sV.tag()
                wt2 = self.dmas("gpsimd", cache[i], eff[:], sem=effsem[j],
                                waits=[self.sV.tag()])
                self.rset(effg[j], wt2)
        qend = [self.sO.tag(), self.sV.tag()]
        self.rset("cfp0", qend[0])
        self.rset("cfp1", qend[0])
        self.rset("endD_P", qend[0])
        for s in ("WA0", "WA1"):
            self.rguard[s] = last_eff[0]
        for s in ("WA2", "WA3"):
            self.rguard[s] = last_eff[1]
        self.rguard["WB0"] = last_eff[2]
        self.rguard["WB1"] = last_eff[3]
        for s in ("sT0", "sT1", "ktr0", "ktr1"):
            self.rguard[s] = self.rguard["eff2"]
        for s in ("vtr0", "vtr1", "qlam0", "qlam1"):
            self.rguard[s] = self.rguard["eff3"]
        self.cast_tasks = []
        self.cast_i = 0
        self.cast_done_tags = []
        for grp, cache in (("gT", self.cgw), ("uT", self.cuw)):
            for i in range(64):
                self.cast_tasks.append((d[grp][i], cache[i]))
        for i in range(32):
            for hh in range(2):
                self.cast_tasks.append((d["dT"][i][:, hh * C:(hh + 1) * C],
                                        self.cdw[i][:, hh * C:(hh + 1) * C]))
        self.quant_done_W = [self.sSMB[0].tag(), self.sSMB[1].tag(),
                             self.sCST[0].tag(), self.sCST[1].tag()]
        if "cqkv" in self.dbg:
            t = self.nc.dram_tensor("dbg_cqkv", [48, 128, C], BF16, kind="ExternalOutput").ap()
            self.dbg_tensors["cqkv"] = t
            for i in range(48):
                self.dma("scalar", t[i], self.cqkv[i], sem=self.sDBG,
                         waits=self.quant_done_W)
        if "cdw" in self.dbg:
            t = self.nc.dram_tensor("dbg_cdw", [32, 128, 4096], BF16, kind="ExternalOutput").ap()
            self.dbg_tensors["cdw"] = t
            for i in range(32):
                self.dma("scalar", t[i], self.cdw[i], sem=self.sDBG,
                         waits=self.quant_done_W)

    def _emit_casts(self, n):
        for _ in range(min(n, len(self.cast_tasks))):
            srcap, dstap = self.cast_tasks.pop(0)
            i = self.cast_i
            self.cast_i += 1
            p4, p2 = i % 3, i % 4
            wt = self.dmas(("sync", "scalar")[i % 2], self.cfp[p4][:], srcap,
                           sem=self.sCLD[p4], waits=self.rw(f"cfp{p4}"))
            ceng = ("vector", "gpsimd", "scalar")[i % 3]
            sem = {"vector": self.sV, "gpsimd": self.sO, "scalar": self.sA}[ceng]
            if ceng == "scalar":
                self.acopy(self.cbf[p2][:], self.cfp[p4][:],
                           waits=[wt] + self.rw(f"cbf{p2}"), sem=sem)
            else:
                self.cpy(ceng, self.cbf[p2][:], self.cfp[p4][:],
                         waits=[wt] + self.rw(f"cbf{p2}"), sem=sem)
            ctag = sem.tag()
            self.rset(f"cfp{p4}", ctag)
            wt2 = self.dmas("gpsimd" if ceng == "vector" else ceng, dstap,
                            self.cbf[p2][:], sem=self.sCST[p2], waits=[ctag])
            self.rset(f"cbf{p2}", wt2)
        if not self.cast_tasks:
            self.cast_done_tags = [s.tag() for s in self.sCST]

    def _ln_coeffs(self, xa, eps, extra_rms=None):
        st = self.stn
        self.red("vector", st[0][:], xa, sem=self.sV)
        self.stt("vector", self.trash[:], xa, 1.0, xa, ALU.bypass, ALU.mult,
                 accum=st[1][:, 0:1], sem=self.sV)
        self.vfence()
        self.ts("vector", st[2][:], st[0][:], 1.0 / C, ALU.mult, sem=self.sV)
        self.vfence()
        k = 1.0 if extra_rms is None else 1.0 + extra_rms
        bias = eps if extra_rms is None else eps * extra_rms
        self.stt("vector", st[4][:], st[2][:], k, st[2][:], ALU.mult, ALU.mult,
                 sem=self.sV)
        self.vfence()
        self.stt("vector", st[5][:], st[1][:], k / C, st[4][:], ALU.mult,
                 ALU.subtract, sem=self.sV)
        self.vfence()
        vv = self.sV.tag()
        self.act(st[7][:], st[5][:], AF.Ln, bias=bias, scale=1.0, waits=[vv], sem=self.sA)
        self.act(st[7][:], st[7][:], AF.Exp, scale=-0.5,
                 waits=[self.sA.tag()], sem=self.sA)
        aa = self.sA.tag()
        self.stt("vector", st[8][:], st[2][:], -1.0, st[7][:], ALU.mult, ALU.mult,
                 waits=[aa], sem=self.sV)
        self.vfence()
        return st[7], st[8]

    def _transpose_to_fm(self, src_tm, dst_fm, tt, tag_wait, extra_evac_waits=()):
        for half in range(2):
            bank = 6 + half
            bwt = self.bw(bank)
            for j in range(8):
                kc = half * 8 + j
                self.tr(self.pt[half][:, j * 128:(j + 1) * 128],
                        src_tm[:, kc * 128:(kc + 1) * 128],
                        waits=([tag_wait] + bwt) if j == 0 else (),
                        sem=self.sP if j == 7 else None)
            mt = self.sP.tag()
            dst = dst_fm[:].rearrange("p (kc t) -> p kc t", t=MT)[:, half * 8:(half + 1) * 8,
                                                                 tt * 128:(tt + 1) * 128]
            self.cpy("vector", dst, self.pt[half][:],
                     waits=[mt] + list(extra_evac_waits), sem=self.sV)
            self.bset(bank, self.sV.tag())

    def emit_phaseA(self, sc, prologue):
        d = self.dr
        xsrc = d["xprev"] if prologue else d["x"]
        for tt in range(NTT):
            par = tt % 2
            row0 = sc * MT + tt * 128
            wt = self.dmas("scalar", self.xt[par][:], xsrc[row0:row0 + 128, :], sem=self.sXT,
                           waits=self.rw("xtg"))
            self.ins("vector", (lambda e: e.engine_nop()), waits=[wt])
            r_ap, b_ap = self._ln_coeffs(self.xt[par][:], 1e-5, extra_rms=EPS32)
            self.ts("vector", self.smb[par][:], self.xt[par][:],
                    r_ap[:, 0:1], ALU.mult, b_ap[:, 0:1], ALU.add,
                    waits=self.rw(f"smb{par}"), sem=self.sV)
            self.rset("xtg", self.sV.tag())
            self._transpose_to_fm(self.smb[par][:], self.h_fm, tt, self.sV.tag(),
                                  extra_evac_waits=self.rw("endD_P"))
            self.rset(f"smb{par}", self.sV.tag())
        h_ready = self.sV.tag()

        firstv = True
        for half in range(2):
            for kc in range(KC):
                par = kc % 4
                slab = self.wa[par]
                wt = self.dmas(("sync", "scalar")[par % 2], slab[:], self.cv[kc],
                               sem=self.sWA[par],
                               waits=self.slw((f"WA{par}",)) +
                                     (self.quant_done_W
                                      if (prologue and half == 0 and kc < 2) else []))
                firstv = False
                for t8 in range(8):
                    tt = half * 2 + t8 // 4
                    hg = t8 % 4
                    dst = self.pf[t8][:] if t8 < 6 else self.pfd[t8 - 6][:]
                    w = []
                    if t8 == 0:
                        w += [wt, h_ready] if kc == 0 else [wt]
                    if kc == 0:
                        w += self.bw(t8)
                    self.mm(dst, self.h_fm[:, kc * MT + tt * 128:kc * MT + (tt + 1) * 128],
                            slab[:, hg * 512:(hg + 1) * 512],
                            kc == 0, kc == KC - 1,
                            waits=w, sem=self.sP if t8 == 7 else None)
                self.slset((f"WA{par}",), self.sP.tag())
            vt = self.sP.tag()
            for t8 in range(8):
                tt = half * 2 + t8 // 4
                hg = t8 % 4
                src = self.pf[t8][:] if t8 < 6 else self.pfd[t8 - 6][:]
                self.cpy("vector", self.v_tm[:, (tt * 4 + hg) * 512:(tt * 4 + hg + 1) * 512],
                         src, waits=[vt] if t8 == 0 else (), sem=self.sV)
                self.bset(t8, self.sV.tag())

        rng_ncn = range(16, 32) if prologue else range(32)
        dsts = {0: self.q_fm, 1: self.k_fm}
        first = True
        for ncn in rng_ncn:
            par = ncn % 4
            slab = self.wa[par]
            wt = self.dmas(("sync", "scalar")[par % 2], slab[:], self.cqkv[ncn],
                           sem=self.sWA[par], waits=self.slw((f"WA{par}",)))
            bank = ncn % 3
            for kc in range(KC):
                self.mm(self.pf[bank][:], slab[:, kc * 128:(kc + 1) * 128],
                        self.h_fm[:, kc * MT:kc * MT + MT],
                        kc == 0, kc == KC - 1,
                        waits=([wt] + self.bw(bank)) if kc == 0 else (),
                        sem=self.sP if kc == KC - 1 else None)
            self.slset((f"WA{par}",), self.sP.tag())
            hd = ncn % 16
            self.cpy("vector", dsts[ncn // 16][:, hd * MT:(hd + 1) * MT], self.pf[bank][:],
                     waits=[self.sP.tag()] +
                           ((self.rw("accw") + self.rw("d_act_done")) if first else []) +
                           (self.cast_done_tags
                            if (first and not prologue and sc == 0) else []),
                     sem=self.sV)
            self.bset(bank, self.sV.tag())
            first = False
        self.qkv_ready = self.sV.tag()
        self.rset("hfm_free", self.sP.tag())
        if sc == 0 and not prologue:
            self.dump("h_fm", self.h_fm[:], (128, KC * MT), BF16)
            self.dump("q_fm", self.q_fm[:], (128, H * MT), BF16)
            self.dump("k_fm", self.k_fm[:], (128, H * MT), BF16)
            self.dump("v_tm", self.v_tm[:], (128, H * MT), BF16)

    def emit_phaseB(self, sc, prologue, gchunk0):
        sbf_tag = [None] * NCH
        for ch in range(NCH + (0 if prologue else 1)):
            if ch < NCH:
                sslot = ch % 2
                if not prologue:
                    self.cpy("vector", self.Sbf[sslot][:], self.S[:],
                             waits=self.rw(f"Sbf{sslot}"), sem=self.sV)
                    sbf_tag[ch] = self.sV.tag()
                for g in range(4):
                    hp = g % 2
                    h0 = 4 * g
                    bank6 = 6 + hp
                    bwt = self.bw(bank6)
                    for j in range(4):
                        h = h0 + j
                        self.tr(self.pt[hp][:, j * 128:(j + 1) * 128],
                                self.k_fm[:, h * MT + ch * 128:h * MT + (ch + 1) * 128],
                                waits=(bwt + [self.qkv_ready]) if j == 0 else (),
                                sem=self.sP if j == 3 else None)
                    trt = self.sP.tag()
                    ktr3 = self.atts["ktr"][hp][:].rearrange("p (h d) -> p h d", d=128)
                    pt3 = self.pt[hp][:, 0:512].rearrange("p (h d) -> p h d", d=128)
                    lr3 = self.lam_rev16[:, h0:h0 + 4][:, :, None].broadcast_to([128, 4, 128])
                    self.tt("vector", ktr3, pt3, lr3, ALU.mult,
                            waits=[trt] + self.rw(f"ktr{hp}"), sem=self.sV)
                    self.bset(bank6, self.sV.tag())
                    ktag = self.sV.tag()
                    kbank = 4 + hp
                    vbase = (ch * 4 + g) * 512
                    for j in range(4):
                        self.mm(self.pf[kbank][:, j * 128:(j + 1) * 128],
                                self.atts["ktr"][hp][:, j * 128:(j + 1) * 128],
                                self.v_tm[:, vbase + j * 128:vbase + (j + 1) * 128],
                                True, True,
                                waits=([ktag] + self.bw(kbank)) if j == 0 else (),
                                sem=self.sP if j == 3 else None)
                    kt = self.sP.tag()
                    self.rset(f"ktr{hp}", kt)
                    for j in range(4):
                        h = h0 + j
                        self.stt("vector", self.S[:, h * 128:(h + 1) * 128],
                                 self.S[:, h * 128:(h + 1) * 128], self.lamT16[:, h:h + 1],
                                 self.pf[kbank][:, j * 128:(j + 1) * 128], ALU.mult, ALU.add,
                                 waits=[kt] if j == 0 else (),
                                 sem=self.sV if j == 3 else None)
                    self.bset(kbank, self.sV.tag())
            if prologue or ch == 0:
                continue
            oc = ch - 1
            sslot = oc % 2
            o_par = oc % 2
            for g in range(4):
                hp = g % 2
                h0 = 4 * g
                for j in range(4):
                    h = h0 + j
                    self.mm(self.pf[hp][:, j * 128:(j + 1) * 128],
                            self.k_fm[:, h * MT + oc * 128:h * MT + (oc + 1) * 128],
                            self.q_fm[:, h * MT + oc * 128:h * MT + (oc + 1) * 128],
                            True, True,
                            waits=self.bw(hp) if j == 0 else (),
                            sem=self.sP if j == 3 else None)
                self.tt("vector", self.atts["sT"][hp][:], self.pf[hp][:],
                        self.masksT[:, h0 * 128:(h0 + 4) * 128], ALU.mult,
                        waits=[self.sP.tag()] + self.rw(f"sT{hp}"), sem=self.sV)
                self.bset(hp, self.sV.tag())
                qsl = self.q_fm[:].rearrange("p (h t) -> p h t", t=MT)[:, h0:h0 + 4,
                                                                      oc * 128:(oc + 1) * 128]
                lam4 = self.lam_t_bc[:].rearrange("p (h t) -> p h t", t=128)[:, h0:h0 + 4, :]
                qlam4 = self.atts["qlam"][hp][:].rearrange("p (h t) -> p h t", t=128)
                self.tt("vector", qlam4, qsl, lam4, ALU.mult,
                        waits=self.rw(f"qlam{hp}"), sem=self.sV)
                vready = self.sV.tag()
                obank = 2 + hp
                vbase = (oc * 4 + g) * 512
                for j in range(4):
                    h = h0 + j
                    self.mm(self.pf[obank][:, j * 128:(j + 1) * 128],
                            self.atts["sT"][hp][:, j * 128:(j + 1) * 128],
                            self.v_tm[:, vbase + j * 128:vbase + (j + 1) * 128],
                            True, False,
                            waits=([vready, sbf_tag[oc]] + self.bw(obank))
                                  if j == 0 else ())
                    self.mm(self.pf[obank][:, j * 128:(j + 1) * 128],
                            self.atts["qlam"][hp][:, j * 128:(j + 1) * 128],
                            self.Sbf[sslot][:, h * 128:(h + 1) * 128],
                            False, True, sem=self.sP if j == 3 else None)
                ot = self.sP.tag()
                self.rset(f"sT{hp}", ot)
                self.rset(f"qlam{hp}", ot)
                if g == 3:
                    self.rset(f"Sbf{sslot}", ot)
                self.acopy(self.o_sb[o_par][:, h0 * 128:(h0 + 4) * 128],
                           self.pf[obank][:],
                           waits=[ot] + ((self.rw(f"osb{o_par}") + self.rw("hfm_free"))
                                         if g == 0 else []),
                           sem=self.sA)
                self.bset(obank, self.sA.tag())

            if True:
                oa = self.o_sb[o_par][:]
                st = self.stn
                self.stt("vector", self.trash[:], oa, 1.0, oa, ALU.bypass, ALU.mult,
                         accum=st[0][:, 0:1], waits=[self.sA.tag()], sem=self.sV)
                self.tt("vector", self.trash[:], oa, self.nw2_bc[:], ALU.mult, sem=self.sV)
                self.stt("vector", self.trash[:], self.trash[:], 1.0, oa, ALU.bypass, ALU.mult,
                         accum=st[1][:, 0:1], sem=self.sV)
                self.vfence()
                self.ts("vector", st[2][:], st[0][:], EPS32 / C, ALU.mult,
                        EPS32 * 1e-6, ALU.add, sem=self.sV)
                self.vfence()
                self.stt("vector", st[6][:], st[1][:], 1.0 / C, st[2][:],
                         ALU.mult, ALU.add, sem=self.sV)
                self.vfence()
                vv = self.sV.tag()
                self.act(st[7][:], st[6][:], AF.Ln, waits=[vv], sem=self.sA)
                self.act(st[7][:], st[7][:], AF.Exp, scale=-0.5,
                         waits=[self.sA.tag()], sem=self.sA)
                par_s = oc % 2
                self.ts("vector", self.smb[par_s][:], oa, st[7][:, 0:1], ALU.mult,
                        waits=[self.sA.tag()] + self.rw(f"smb{par_s}"), sem=self.sV)
                self.rset(f"osb{o_par}", self.sV.tag())
                self._transpose_to_fm(self.smb[par_s][:], self.fm2, oc, self.sV.tag(),
                                      extra_evac_waits=self.rw("endD_P"))
                self.rset(f"smb{par_s}", self.sV.tag())
        if not prologue:
            self.out2_ready = self.sV.tag()
            self.rset("endB_P", self.sP.tag())
            if sc == 0:
                self.dump("o_sc0", self.o_sb[1][:], (128, C))
                self.dump("out2_fm", self.fm2[:], (128, KC * MT), BF16)
                self.dump("S_sc0", self.S[:], (128, H * 128))
        elif sc == self.n_prol - 1:
            self.dump("S_init", self.S[:], (128, H * 128))

    def emit_phaseC(self, sc):
        d = self.dr
        for ncn in range(16):
            par = ncn % 4
            slab = self.wa[par]
            wt = self.dmas(("sync", "scalar")[par % 2], slab[:], self.cpw[ncn],
                           sem=self.sWA[par], waits=self.slw((f"WA{par}",)))
            bank = ncn % 3
            for kc in range(KC):
                self.mm(self.pf[bank][:], slab[:, kc * 128:(kc + 1) * 128],
                        self.fm2[:, kc * MT:kc * MT + MT],
                        kc == 0, kc == KC - 1,
                        waits=([wt, self.out2_ready] + self.bw(bank))
                              if kc == 0 else (),
                        sem=self.sP if kc == KC - 1 else None)
            self.slset((f"WA{par}",), self.sP.tag())
            self.cpy("vector", self.attn_fm[:, ncn * MT:(ncn + 1) * MT], self.pf[bank][:],
                     waits=[self.sP.tag()] + (self.rw("endB_P") if ncn == 0 else []),
                     sem=self.sV)
            self.bset(bank, self.sV.tag())
        proj_done = self.sV.tag()

        for tt in range(NTT):
            par = tt % 2
            row0 = sc * MT + tt * 128
            xw = self.dmas("scalar", self.xt[par][:], d["x"][row0:row0 + 128, :], sem=self.sXT,
                           waits=self.rw("xtg"))
            att = self.attn_sb[par]
            for qb in range(4):
                bank = 3 + qb % 2
                bwt = self.bw(bank)
                for j in range(4):
                    ncn = qb * 4 + j
                    self.tr(self.pf[bank][:, j * 128:(j + 1) * 128],
                            self.attn_fm[:, ncn * MT + tt * 128:ncn * MT + (tt + 1) * 128],
                            ident=self.identf[:],
                            waits=(bwt + [proj_done]) if j == 0 else (),
                            sem=self.sP if j == 3 else None)
                self.cpy("vector", att[:, qb * 512:(qb + 1) * 512], self.pf[bank][:],
                         waits=[self.sP.tag()] +
                               (self.rw(f"attn{par}") if qb == 0 else []),
                         sem=self.sV)
                self.bset(bank, self.sV.tag())
            r_ap, b_ap = self._ln_coeffs(att[:], 1e-5)
            self.ts("vector", self.trash[:], att[:], r_ap[:, 0:1], ALU.mult,
                    b_ap[:, 0:1], ALU.add, sem=self.sV)
            self.rset(f"attn{par}", self.sV.tag())
            self.vfence()
            self.tt("vector", self.x2t[:], self.trash[:], self.xt[par][:], ALU.add,
                    waits=[xw] + self.rw("x2t") + self.rw("x2t_store"), sem=self.sV)
            self.rset("xtg", self.sV.tag())
            x2w = self.dmas("scalar", self.x2s[row0:row0 + 128, :], self.x2t[:], sem=self.sX2,
                            waits=[self.sV.tag()])
            self.rset("x2t_store", x2w)
            r2, b2 = self._ln_coeffs(self.x2t[:], 1e-5)
            par_s = tt % 2
            self.ts("vector", self.smb[par_s][:], self.x2t[:], r2[:, 0:1], ALU.mult,
                    b2[:, 0:1], ALU.add, waits=self.rw(f"smb{par_s}"), sem=self.sV)
            self.rset("x2t", self.sV.tag())
            self._transpose_to_fm(self.smb[par_s][:], self.fm2, tt, self.sV.tag())
            self.rset(f"smb{par_s}", self.sV.tag())
            if sc == 0 and tt == 0:
                self.dump("attn_t0", att[:], (128, C))
                self.dump("x2_t0", self.x2t[:], (128, C))
        self.ln2_ready = self.sV.tag()
        self.rset("endC_P", self.sP.tag())
        if sc == 0:
            self.dump("ln2_fm", self.fm2[:], (128, KC * MT), BF16)

    def emit_phaseD(self, sc):
        hsb_r = self.h_sbuf[:].rearrange("p (c t) -> p c t", t=MT)
        for s4 in range(4):
            for cgl in range(4):
                cg = s4 * 4 + cgl
                for r in range(4):
                    par = r % 2
                    gs, us = self.wa[par * 2], self.wa[par * 2 + 1]
                    cdone = self.cast_done_tags if (sc == 0 and cg == 0 and r == 0) else []
                    gw = self.dmas("sync", gs[:], self.cgw[cg * 4 + r], sem=self.sWA[2 * par],
                                   waits=self.slw((f"WA{2*par}",)) + cdone)
                    uw = self.dmas("scalar", us[:], self.cuw[cg * 4 + r],
                                   sem=self.sWA[2 * par + 1],
                                   waits=self.slw((f"WA{2*par+1}",)) + cdone)
                    for kc in range(KC):
                        self.mm(self.pf[par][:], gs[:, kc * 128:(kc + 1) * 128],
                                self.fm2[:, kc * MT:kc * MT + MT], kc == 0, kc == KC - 1,
                                waits=([gw, self.ln2_ready] + self.bw(par))
                                      if kc == 0 else (),
                                sem=self.sP if kc == KC - 1 else None)
                    gt = self.sP.tag()
                    for kc in range(KC):
                        self.mm(self.pf[2 + par][:], us[:, kc * 128:(kc + 1) * 128],
                                self.fm2[:, kc * MT:kc * MT + MT], kc == 0, kc == KC - 1,
                                waits=([uw] + self.bw(2 + par)) if kc == 0 else (),
                                sem=self.sP if kc == KC - 1 else None)
                    ut = self.sP.tag()
                    self.slset((f"WA{2*par}", f"WA{2*par+1}"), ut)
                    self.act(self.sg[par][:], self.pf[par][:], AF.Silu,
                             waits=[gt] + self.rw(f"sg{par}"), sem=self.sA)
                    self.bset(par, self.sA.tag())
                    self.tt("vector", self.O_r[:, r * MT:(r + 1) * MT], self.pf[2 + par][:],
                            self.sg[par][:], ALU.mult,
                            waits=[ut, self.sA.tag()] +
                                  ((self.rw("Or") + self.rw("x2t_store")) if r == 0 else []),
                            sem=self.sV)
                    self.rset(f"sg{par}", self.sV.tag())
                    self.bset(2 + par, self.sV.tag())
                vg = self.sV.tag()
                for r in range(4):
                    self.act(self.a_r[:, r * MT:(r + 1) * MT], self.O_r[:, r * MT:(r + 1) * MT],
                             AF.Abs, waits=([vg] + self.rw("ar")) if r == 0 else (), sem=self.sA)
                at = self.sA.tag()
                A = lambda r: self.a_r[:, r * MT:(r + 1) * MT]
                SP = lambda j: self.sp_r[:, j * MT:(j + 1) * MT]
                self.tt("vector", SP(0), A(0), A(1), ALU.max, waits=[at], sem=self.sV)
                self.tt("vector", SP(1), A(0), A(1), ALU.min, sem=self.sV)
                self.tt("vector", SP(2), A(2), A(3), ALU.max, sem=self.sV)
                self.tt("vector", SP(3), A(2), A(3), ALU.min, sem=self.sV)
                self.tt("vector", SP(0), SP(0), SP(2), ALU.min, sem=self.sV)
                self.tt("vector", SP(1), SP(1), SP(3), ALU.max, sem=self.sV)
                self.tt("vector", SP(0), SP(0), SP(1), ALU.max, sem=self.sV)
                for r in range(4):
                    self.tt("vector", SP(2), A(r), SP(0), ALU.is_ge, sem=self.sV)
                    self.tt("vector", self.h_sbuf[:, (cgl * 4 + r) * MT:(cgl * 4 + r + 1) * MT],
                            self.O_r[:, r * MT:(r + 1) * MT], SP(2), ALU.mult,
                            waits=(self.rw("hsb") + self.rw("endC_P"))
                                  if (cgl == 0 and r == 0) else (), sem=self.sV)
                self.rset("Or", self.sV.tag())
                self.rset("ar", self.sV.tag())
            hs_ready = self.sV.tag()
            if sc == 0 and s4 == 0:
                self.dump("hsb_sc0", self.h_sbuf[:], (128, 16 * MT), BF16)
                self.dump("Or_sc0", self.O_r[:], (128, 4 * MT))
                self.dump("ar_sc0", self.a_r[:], (128, 4 * MT))

            for n2h in range(2):
                for qtr in range(4):
                    par = qtr % 2
                    slab = self.wbq[2 * n2h + par]
                    dwt = self.dmas(("sync", "scalar")[(2 * n2h + par) % 2], slab[:],
                                    self.cdw[s4 * 8 + n2h * 4 + qtr],
                                    sem=self.sWB[2 * n2h + par],
                                    waits=self.slw((f"WB{2 * n2h + par}",)) +
                                          (self.cast_done_tags if (sc == 0 and s4 == 0 and
                                                                   n2h == 0 and qtr == 0) else []))
                    first_w = [dwt, hs_ready]
                    for tt in range(NTT):
                        for ic in range(4):
                            gic = qtr * 4 + ic
                            lhs = hsb_r[:, gic, tt * 128:(tt + 1) * 128]
                            for n2 in range(2):
                                bank = tt * 2 + n2
                                self.mm(self.pf[bank][:] if bank < 6 else self.pfd[bank - 6][:],
                                        lhs,
                                        slab[:, ic * 1024 + n2 * 512:ic * 1024 + (n2 + 1) * 512],
                                        start=(qtr == 0 and ic == 0), stop=(qtr == 3 and ic == 3),
                                        waits=(first_w + (self.bw(bank) if (qtr == 0 and ic == 0) else []))
                                              if (ic == 0) else (),
                                        sem=self.sP if (tt == 3 and ic == 3 and n2 == 1) else None)
                    self.slset((f"WB{2 * n2h + par}",), self.sP.tag())
                mt = self.sP.tag()
                for tt in range(NTT):
                    for n2 in range(2):
                        bank = tt * 2 + n2
                        src = self.pf[bank][:] if bank < 6 else self.pfd[bank - 6][:]
                        dst = self.acc[:, tt * C + (n2h * 2 + n2) * 512:
                                       tt * C + (n2h * 2 + n2 + 1) * 512]
                        if s4 == 0:
                            self.cpy("vector", dst, src,
                                     waits=([mt] + self.rw("accw")) if (tt == 0 and n2 == 0) else (),
                                     sem=self.sV)
                        else:
                            self.tt("vector", dst, dst, src, ALU.add,
                                    waits=[mt] if (tt == 0 and n2 == 0) else (), sem=self.sV)
                        self.bset(bank, self.sV.tag())
            self.rset("hsb", self.sP.tag())
            if sc == 0 and s4 == 0:
                self.dump("acc0_sc0", self.acc[:], (128, NTT * C))
        if sc == 0:
            self.dump("acc_sc0", self.acc[:], (128, NTT * C))
        for tt in range(NTT):
            par = tt % 2
            row0 = sc * MT + tt * 128
            xw = self.dmas("scalar", self.xt[par][:], self.x2s[row0:row0 + 128, :], sem=self.sXT,
                           waits=self.rw("xtg"))
            atile = self.acc[:, tt * C:(tt + 1) * C]
            r_ap, b_ap = self._ln_coeffs(atile, 1e-5)
            self.ts("vector", self.trash[:], atile, r_ap[:, 0:1], ALU.mult,
                    b_ap[:, 0:1], ALU.add, sem=self.sV)
            self.vfence()
            self.tt("vector", atile, self.trash[:], self.xt[par][:], ALU.add,
                    waits=[xw], sem=self.sV)
            self.rset("xtg", self.sV.tag())
            self.dmas("scalar", self.out[row0:row0 + 128, :], atile, sem=self.sOUT,
                      waits=[self.sV.tag()])
            if sc == 0 and tt == 0:
                self.dump("out_t0", atile, (128, C))
        self.rset("accw", self.sOUT.tag())
        self.rset("endD_P", self.sP.tag())
        self.rset("d_act_done", self.sA.tag())

    def build(self):
        self.emit_setup()
        if self.scope in ("full", "quant", "qc"):
            self.emit_quant()
        if self.scope == "qc":
            self._emit_casts(10 ** 9)
            if "cgw" in self.dbg:
                t = self.nc.dram_tensor("dbg_cgw", [64, 128, C], BF16,
                                        kind="ExternalOutput").ap()
                self.dbg_tensors["cgw"] = t
                for i in range(64):
                    self.dma("scalar", t[i], self.cgw[i], sem=self.sDBG,
                             waits=self.cast_done_tags)
            if "cdw2" in self.dbg:
                t = self.nc.dram_tensor("dbg_cdw2", [32, 128, 4096], BF16,
                                        kind="ExternalOutput").ap()
                self.dbg_tensors["cdw2"] = t
                for i in range(32):
                    self.dma("scalar", t[i], self.cdw[i], sem=self.sDBG,
                             waits=self.cast_done_tags)
        if self.scope == "full":
            npsc = max(1, self.n_prol)
            per = (len(self.cast_tasks) + npsc - 1) // npsc
            for psc in range(self.n_prol):
                self.emit_phaseA(psc, prologue=True)
                self._emit_casts(per // 2)
                self.emit_phaseB(psc, prologue=True, gchunk0=psc * NCH)
                self._emit_casts(per - per // 2)
            self._emit_casts(10 ** 9)
            g0 = self.n_prol * NCH
            for sc in range(self.n_main):
                self.emit_phaseA(sc, prologue=False)
                self.emit_phaseB(sc, prologue=False, gchunk0=g0 + sc * NCH)
                self.emit_phaseC(sc)
                self.emit_phaseD(sc)
        fin = [(s.h, s.n) for s in self.all_sems if s.n > 0]
        clr = [s.h for s in self.all_sems]
        def f(e, fin=tuple(fin), clr=tuple(clr)):
            for hh, v in fin:
                e.wait_ge(hh, v)
            for hh in clr:
                e.sem_clear(hh)
        self._app("sync", f)

        nc = self.nc
        with nc.Block() as blk:
            @blk.sync
            def _(e):
                for fn in self.st["sync"]:
                    fn(e)

            @blk.tensor
            def _(e):
                for fn in self.st["tensor"]:
                    fn(e)

            @blk.vector
            def _(e):
                for fn in self.st["vector"]:
                    fn(e)

            @blk.scalar
            def _(e):
                for fn in self.st["scalar"]:
                    fn(e)

            @blk.gpsimd
            def _(e):
                for fn in self.st["gpsimd"]:
                    fn(e)
        return nc


def kernel(**inputs):
    in_maps = _host_prep(inputs)
    nc = KB().build()
    res = run_bass_kernel_spmd(nc, in_maps, core_ids=list(range(N_CORES)))
    out = np.empty((B_, SEQ, C), np.float32)
    for core in range(N_CORES):
        b, hf = core // 2, core % 2
        out[b, hf * M:(hf + 1) * M] = res.results[core]["out"]
    return out
